# version 05
# baseline (speedup 1.0000x reference)
import math
import contextlib
import numpy as np

import concourse.bass as bass
import concourse.mybir as mybir
from concourse.bass_utils import run_bass_kernel_spmd

F32 = mybir.dt.float32
AF = mybir.ActivationFunctionType
ALU = mybir.AluOpType
AX = mybir.AxisListType

B, S, D = 4, 4096, 1024
NDC = D // 128
E = 2048
SDIM = 128
FUV = 2 * E + SDIM
HEADS, DH = 8, 64
INNER = HEADS * DH
WIN = 512
HID = 2730
HID2 = 2 * HID
NHC = (HID + 127) // 128
EPS_LN = 1e-5
NEG = -3.4028235e38
LN1E4 = math.log(10000.0)
PI = math.pi

TOWN = 2048
NT = 28
TB = NT * 128
OWN0 = 12
BAND = 5
NKB = BAND * 128
QKV_T = {0: 1, 1: 7}
ATTN_T = {0: 5, 1: 11}
FFN_T = {0: 6, 1: 12}

ENGINES = ("sync", "tensor", "vector", "scalar", "gpsimd")


class _Sem:
    def __init__(self, prog, name, dma):
        self.name, self.idx = name, len(prog.sems)
        self.step = 16 if dma else 1
        self.count = 0
        prog.sems.append(self)


class _Queue:
    def __init__(self, prog, engine):
        self.prog, self.engine = prog, engine

    def wait(self, *toks):
        merged = {}
        for tok in toks:
            if tok is None:
                continue
            si, val = tok
            merged[si] = max(merged.get(si, 0), val)
        w = self.prog.engine_waited[self.engine]
        for si, val in merged.items():
            if w.get(si, 0) < val:
                w[si] = val
                self.prog.engine_entries[self.engine].append(("wait", si, val))

    def emit(self, fn):
        self.prog.engine_entries[self.engine].append(("emit", fn))

    def emitm(self, sem, fn):
        sem.count += sem.step
        self.prog.engine_entries[self.engine].append(("emitm", fn, sem.idx, sem.step))
        return (sem.idx, sem.count)


class _Prog:
    def __init__(self, nc):
        self.nc = nc
        self.sems = []
        self.engine_entries = {e: [] for e in ENGINES}
        self.engine_waited = {e: {} for e in ENGINES}
        self._queues = {}

    def sem(self, name, dma=False):
        return _Sem(self, name, dma)

    def q(self, engine):
        if engine not in self._queues:
            self._queues[engine] = _Queue(self, engine)
        return self._queues[engine]

    def barrier(self):
        toks = [(s.idx, s.count) for s in self.sems if s.count]
        for e in ENGINES:
            self.q(e).wait(*toks)

    def build(self):
        nc = self.nc
        with contextlib.ExitStack() as stack:
            handles = [stack.enter_context(nc.semaphore(f"s{i}_{s.name}"))
                       for i, s in enumerate(self.sems)]
            block = stack.enter_context(nc.Block())

            def section(eng_name):
                entries = self.engine_entries[eng_name]
                def body(eng):
                    for e in entries:
                        if e[0] == "wait":
                            eng.wait_ge(handles[e[1]], e[2])
                        elif e[0] == "emit":
                            e[1](eng)
                        else:
                            e[1](eng).then_inc(handles[e[2]], e[3])
                return body

            for e in ENGINES:
                if self.engine_entries[e]:
                    getattr(block, e)(section(e))


def build_full(debug=None):
    nc = bass.Bass("TRN2", target_bir_lowering=False)
    d = lambda n, shp, kind="ExternalInput": nc.dram_tensor(n, list(shp), F32, kind=kind).ap()
    x_attn = d("x_attn", [S, D])
    x_bwin = d("x_bwin", [TB, D])
    w_uv = d("w_uv", [D, FUV])
    b_uv = d("b_uv", [1, FUV])
    gbat = d("gbat", [128, 4])
    w_out = d("w_out", [E, D])
    b_out = d("b_out", [1, D])
    wqkv_h = d("wqkv", [2, D, 3 * INNER])
    wo_h = d("wo", [2, INNER, D])
    w1_h = d("w1", [2, D, HID2])
    b1_h = d("b1", [2, HID2])
    w2_h = d("w2", [2, HID, D])
    b2_h = d("b2", [2, D])
    lngb = d("lngb", [128, 32])
    qs_row = d("qs_row", [2, DH])
    ks_row = d("ks_row", [2, DH])
    qs_sh = d("qs_sh", [2, DH])
    ks_sh = d("ks_sh", [2, DH])
    eps_row = d("eps_row", [1, DH])
    cos_h = d("cos_t", [TB, DH])
    sin_h = d("sin_t", [TB, DH])
    scl_h = d("scale_t", [TB, DH])
    masks_h = {0: d("masks0", [NT - ATTN_T[0], 128, NKB]),
               1: d("masks1", [NT - ATTN_T[1], 128, NKB])}
    ident_h = d("ident", [128, 128])
    kill_h = d("kill01", [128, 1])
    out_h = d("out", [TOWN, D], kind="ExternalOutput")
    xn_hbm = d("xn_scr", [D, S], kind="Internal")
    v_hbm = d("v_scr", [S, E], kind="Internal")
    a5_hbm = d("a5_scr", [D, TOWN], kind="Internal")
    if debug:
        dbg_shp = {"xn": [128, NDC * S], "kq": [128, S + TOWN], "v": [S, E],
                   "a5": [D, TOWN], "xb0": [128, NDC * TB],
                   "xb1": [128, NDC * TB], "xb2": [128, NDC * TB],
                   "xmid0": [128, NDC * TB],
                   "ffn0blk": [128, NDC * 512 + NHC * 512],
                   "xb1blk": [128, NDC * TB],
                   "as0": [128, 256], "as1": [128, 512], "as2": [128, 3 * INNER],
                   "as3": [128, 1024], "as4": [128, NKB], "as5": [128, 512],
                   "as1a": [128, 128], "as1b": [128, 128], "as1c": [128, 128],
                   "as1d": [128, 128], "as1e": [128, 128],
                   "as1x": [128, 128], "as1y": [128, 128], "as1z": [128, 128],
                   "o1": [128, 1024], "o2": [128, 1024], "o3": [128, 1024],
                   "o4": [128, 1024], "o5": [128, 1024], "o6": [128, 1024],
                   "fullplus": [128, NDC * TB + D * 16]}[debug]
        dbg = d("dbg", dbg_shp, kind="ExternalOutput")

    p = _Prog(nc)
    ld, te, ve, sc, gp = (p.q(e) for e in ENGINES)
    s_ld = p.sem("ld", dma=True)
    s_te = p.sem("te")
    s_ve = p.sem("ve")
    s_sc = p.sem("sc")
    s_gp = p.sem("gp")
    s_sxn = p.sem("sxn", dma=True)
    s_sv = [p.sem("sv0", dma=True), p.sem("sv1", dma=True)]
    s_sa5 = [p.sem("sa50", dma=True), p.sem("sa51", dma=True)]
    s_sout = [p.sem("sout0", dma=True), p.sem("sout1", dma=True)]
    s_dbg = p.sem("sdbg", dma=True)
    dms = [p.sem(f"dm{i}", dma=True) for i in range(28)]
    dmi = [0]

    def stream_sem():
        s = dms[dmi[0] % len(dms)]
        dmi[0] += 1
        return s

    ctx = contextlib.ExitStack()
    SB = lambda n, shp: ctx.enter_context(nc.sbuf_tensor(n, list(shp), F32))

    ident = SB("ident_sb", [128, 128])
    ones = SB("ones", [1, 512])
    ones_col = SB("ones_col", [128, 1])
    gb = SB("gb", [128, 4])
    gbq = SB("gbq", [128, 2])
    lng = SB("lng", [128, 32])
    s_const = p.sem("const", dma=True)
    kill_sb = SB("kill_sb", [128, 1])
    ld.emitm(s_const, lambda e: e.dma_start(out=kill_sb[:], in_=kill_h[:, :]))
    ld.emitm(s_const, lambda e: e.dma_start(out=ident[:], in_=ident_h[:, :]))
    ld.emitm(s_const, lambda e: e.dma_start(out=gb[:], in_=gbat[:, :]))
    const_tok = ld.emitm(s_const, lambda e: e.dma_start(out=lng[:], in_=lngb[:, :]))
    i_tok = g_tok = lng_tok = const_tok
    ones_tok = ve.emitm(s_ve, lambda e: e.memset(ones[:], 1.0))
    oc_tok = ve.emitm(s_ve, lambda e: e.memset(ones_col[:], 1.0))
    ve.wait(g_tok)
    gbq_tok = ve.emitm(s_ve, lambda e: e.tensor_scalar_mul(gbq[:], gb[:, 0:2], 1.0 / 4096.0))

    ln_nm = SB("ln_nm", [128, 1])
    ln_ss = SB("ln_ss", [128, 1])
    ln_v1 = SB("ln_v1", [128, 1])
    ln_rs = SB("ln_rs", [128, 1])
    ln_state = {"chain": None, "sq": None}

    def ln_tile(x_ap, dst_ap, deps):
        deps = [t for t in deps if t] + ([ln_state["chain"]] if ln_state["chain"] else [])
        ve.wait(*deps)
        t1 = ve.emitm(s_ve, lambda e: e.reduce_sum(ln_nm[:], x_ap, axis=AX.X, negate=True))
        ve.wait(t1)
        t2 = ve.emitm(s_ve, lambda e: e.tensor_scalar_mul(ln_nm[:], ln_nm[:], 1.0 / D))
        sc.wait(*deps)
        sq_buf = ln_state["sq"]
        t3 = sc.emitm(s_sc, lambda e, sq_buf=sq_buf: e.activation(sq_buf[:], x_ap, AF.Square, accum_out=ln_ss[:]))
        ve.wait(t2, t3)
        t4 = ve.emitm(s_ve, lambda e: e.tensor_scalar_mul(ln_v1[:], ln_ss[:], 1.0 / D))
        t5 = ve.emitm(s_ve, lambda e: e.tensor_mul(ln_rs[:], ln_nm[:], ln_nm[:]))
        ve.wait(t4, t5)
        t6 = ve.emitm(s_ve, lambda e: e.scalar_tensor_tensor(ln_v1[:], ln_rs[:], -1.0, ln_v1[:], ALU.mult, ALU.add))
        ve.wait(t6)
        t7 = ve.emitm(s_ve, lambda e: e.tensor_scalar_add(ln_v1[:], ln_v1[:], EPS_LN))
        sc.wait(t7)
        t8 = sc.emitm(s_sc, lambda e: e.activation(ln_rs[:], ln_v1[:], AF.Sqrt))
        ve.wait(t8)
        t9 = ve.emitm(s_ve, lambda e: e.reciprocal(ln_rs[:], ln_rs[:]))
        ve.wait(t9)
        t10 = ve.emitm(s_ve, lambda e: e.tensor_scalar(dst_ap, x_ap, ln_nm[:, 0:1], ln_rs[:, 0:1], ALU.add, ALU.mult))
        ln_state["chain"] = t10
        return t10

    def mm_one(psum_ap, la, ra, start, stop, gate=(), free=None, mark=True):
        te.wait(*[t for t in gate if t])
        if free is not None:
            te.wait(free)
        if mark:
            return te.emitm(s_te, lambda e: e.matmul(psum_ap, la, ra, start=start, stop=stop))
        te.emit(lambda e: e.matmul(psum_ap, la, ra, start=start, stop=stop))
        return None

    def mm_group(psum_ap, pairs, bias=None, gate=(), free=None):
        te.wait(*[t for t in gate if t])
        if free is not None:
            te.wait(free)
        if bias is not None:
            bl, br = bias
            te.emit(lambda e: e.matmul(psum_ap, bl, br, start=True, stop=False))
        n = len(pairs)
        tok = None
        for i, (la, ra) in enumerate(pairs):
            start = (bias is None) and i == 0
            if i == n - 1:
                tok = te.emitm(s_te, lambda e, la=la, ra=ra, start=start: e.matmul(psum_ap, la, ra, start=start, stop=True))
            else:
                te.emit(lambda e, la=la, ra=ra, start=start: e.matmul(psum_ap, la, ra, start=start, stop=False))
        return tok

    def fm_ap(buf, width, off, n, pn=128):
        F = buf.ap().shape[1]
        return bass.AP(tensor=buf.ap().tensor, offset=off, ap=[[F, pn], [width, NDC], [1, n]])

    def dstore(sem, deps, dst_ap, src_ap):
        gp.wait(*[t for t in deps if t])
        return gp.emitm(sem, lambda e: e.dma_start(out=dst_ap, in_=src_ap))

    def finish(last_dma_sems):
        gp.wait(*[(s.idx, s.count) for s in last_dma_sems if s.count])
        p.build()
        return nc

    kq_tok = [None]
    xn_done = [None]

    phA = contextlib.ExitStack()
    k_fm = phA.enter_context(nc.sbuf_tensor("k_fm", [128, S], F32))
    q_fm = phA.enter_context(nc.sbuf_tensor("q_fm", [128, TOWN], F32))
    with contextlib.ExitStack() as sco:
        SBa = lambda n, shp: sco.enter_context(nc.sbuf_tensor(n, list(shp), F32))
        PSa = lambda n, shp: sco.enter_context(nc.psum_tensor(n, list(shp), F32))
        xn_fm = SBa("xn_fm", [128, NDC * S])
        x_t = SBa("x_t", [128, 2 * D])
        xn_t = SBa("xn_t", [128, D])
        ln_state["sq"] = SBa("ln_sq_a", [128, D])
        bb_base = SBa("bb_base", [1, SDIM])
        bv_eg = SBa("bv_eg", [1, 2 * 512])
        wbase = SBa("wbase", [128, NDC * SDIM])
        base_sb = SBa("base_sb", [128, 2 * 512])
        wv8 = SBa("wv8", [128, NDC * 512])
        v_sb = SBa("v_sb", [128, 2 * 512])
        psT = PSa("psT", [128, 1024])
        psA = PSa("psA", [128, 2 * 512])
        psA2 = PSa("psA2", [128, 2 * 512])

        s_ab1 = stream_sem()
        bv_tok = ld.emitm(s_ab1, lambda e: e.dma_start(out=bb_base[:], in_=b_uv[:, 2 * E:2 * E + SDIM]))

        psT_free = None
        prev_tt = None
        sx = [stream_sem(), stream_sem()]
        x_free = [None, None]
        for t in range(S // 128):
            sl = t % 2
            ld.wait(x_free[sl])
            lt = ld.emitm(sx[sl], lambda e, t=t, sl=sl: e.dma_start(
                out=x_t[:, sl * D:(sl + 1) * D], in_=x_attn[t * 128:(t + 1) * 128, :]))
            lnt = ln_tile(x_t[:, sl * D:(sl + 1) * D], xn_t[:], [lt, prev_tt])
            te.wait(lnt, i_tok)
            if psT_free:
                te.wait(psT_free)
            for dc in range(NDC):
                fn = lambda e, dc=dc: e.transpose(
                    psT[:, dc * 128:(dc + 1) * 128], xn_t[:, dc * 128:(dc + 1) * 128], ident[:])
                tt = te.emitm(s_te, fn) if dc == NDC - 1 else te.emit(fn)
            prev_tt = tt
            ve.wait(tt)
            dst = fm_ap(xn_fm, S, t * 128, 128)
            ev = ve.emitm(s_ve, lambda e, dst=dst: e.tensor_copy(dst, psT[:].rearrange("p (dc c) -> p dc c", c=128)))
            psT_free = ev
            x_free[sl] = lnt
            dstore(s_sxn, [ev], xn_hbm.rearrange("(dc p) t -> p dc t", p=128)[:, :, t * 128:(t + 1) * 128], dst)
        xn_done[0] = ev

        if debug == "xn":
            dstore(s_dbg, [ev], dbg[:, :], xn_fm[:])
            return finish([s_dbg, s_sxn])

        wb_tok = ld.emitm(stream_sem(), lambda e: e.dma_start(
            out=wbase[:], in_=w_uv.rearrange("(dc p) f -> p dc f", p=128)[:, :, 2 * E:2 * E + SDIM]))
        prev_aff = [None, None]
        for tb in range(S // 512):
            sl = tb % 2
            pairs = [(wbase[:, dc * SDIM:(dc + 1) * SDIM],
                      xn_fm[:, dc * S + tb * 512: dc * S + (tb + 1) * 512]) for dc in range(NDC)]
            mm = mm_group(psA[:, sl * 512:(sl + 1) * 512], pairs,
                          bias=(bb_base[:, :], ones[:, :]),
                          gate=[wb_tok, bv_tok, ones_tok, xn_done[0]], free=prev_aff[sl])
            sc.wait(mm)
            if prev_aff[sl]:
                sc.wait(prev_aff[sl])
            sil = sc.emitm(s_sc, lambda e, sl=sl: e.activation(base_sb[:, sl * 512:(sl + 1) * 512], psA[:, sl * 512:(sl + 1) * 512], AF.Silu))
            ve.wait(sil, gbq_tok)
            kev = ve.emitm(s_ve, lambda e, tb=tb, sl=sl: e.tensor_scalar(
                k_fm[:, tb * 512:(tb + 1) * 512], base_sb[:, sl * 512:(sl + 1) * 512], gb[:, 2:3], gb[:, 3:4], ALU.mult, ALU.add))
            last = kev
            if tb < TOWN // 512:
                last = ve.emitm(s_ve, lambda e, tb=tb, sl=sl: e.tensor_scalar(
                    q_fm[:, tb * 512:(tb + 1) * 512], base_sb[:, sl * 512:(sl + 1) * 512], gbq[:, 0:1], gbq[:, 1:2], ALU.mult, ALU.add))
            prev_aff[sl] = last
        kq_tok[0] = (s_ve.idx, s_ve.count)

        if debug == "kq":
            dstore(s_dbg, [kq_tok[0]], dbg[:, 0:S], k_fm[:])
            dstore(s_dbg, [kq_tok[0]], dbg[:, S:S + TOWN], q_fm[:])
            return finish([s_dbg, s_sxn])

        swv = stream_sem()
        sbv = [stream_sem(), stream_sem()]
        bv_free = [None, None]
        wv_free = None
        sil_prev = [None, None]
        st_prev = [None, None]
        for eg in range(E // 512):
            if wv_free:
                ld.wait(wv_free)
            wv_tok = ld.emitm(swv, lambda e, eg=eg: e.dma_start(
                out=wv8[:], in_=w_uv.rearrange("(dc p) f -> p dc f", p=128)[:, :, E + eg * 512: E + (eg + 1) * 512]))
            egsl = eg % 2
            ld.wait(bv_free[egsl])
            bvt = ld.emitm(sbv[egsl], lambda e, eg=eg, egsl=egsl: e.dma_start(
                out=bv_eg[:, egsl * 512:(egsl + 1) * 512], in_=b_uv[:, E + eg * 512:E + (eg + 1) * 512]))
            for t in range(S // 128):
                sl = t % 2
                pairs = [(xn_fm[:, dc * S + t * 128: dc * S + (t + 1) * 128],
                          wv8[:, dc * 512:(dc + 1) * 512]) for dc in range(NDC)]
                mm = mm_group(psA2[:, sl * 512:(sl + 1) * 512], pairs,
                              bias=(ones[:, 0:128], bv_eg[:, egsl * 512:(egsl + 1) * 512]),
                              gate=[wv_tok, bvt, ones_tok, xn_done[0]], free=sil_prev[sl])
                sc.wait(mm)
                if st_prev[sl]:
                    sc.wait(st_prev[sl])
                sil = sc.emitm(s_sc, lambda e, sl=sl: e.activation(v_sb[:, sl * 512:(sl + 1) * 512], psA2[:, sl * 512:(sl + 1) * 512], AF.Silu))
                sil_prev[sl] = sil
                st_prev[sl] = dstore(s_sv[sl], [sil], v_hbm[t * 128:(t + 1) * 128, eg * 512:(eg + 1) * 512], v_sb[:, sl * 512:(sl + 1) * 512])
            wv_free = mm
            bv_free[egsl] = mm

        if debug == "v":
            gp.wait((s_sv[0].idx, s_sv[0].count), (s_sv[1].idx, s_sv[1].count))
            ld.wait((s_sv[0].idx, s_sv[0].count), (s_sv[1].idx, s_sv[1].count))
            bstate = None
            s_bn = stream_sem()
            for t in range(S // 128):
                for eg in range(E // 512):
                    ld.wait(bstate)
                    lt = ld.emitm(s_bn, lambda e, t=t, eg=eg: e.dma_start(
                        out=v_sb[:, 0:512], in_=v_hbm[t * 128:(t + 1) * 128, eg * 512:(eg + 1) * 512]))
                    bstate = dstore(s_dbg, [lt], dbg[t * 128:(t + 1) * 128, eg * 512:(eg + 1) * 512], v_sb[:, 0:512])
            return finish([s_dbg])

    p.barrier()

    xn_store_all = (s_sxn.idx, s_sxn.count)
    v_store_all0 = (s_sv[0].idx, s_sv[0].count)
    v_store_all1 = (s_sv[1].idx, s_sv[1].count)
    with contextlib.ExitStack() as sco:
        SBa = lambda n, shp: sco.enter_context(nc.sbuf_tensor(n, list(shp), F32))
        PSa = lambda n, shp: sco.enter_context(nc.psum_tensor(n, list(shp), F32))
        st_sb = SBa("st_sb", [128, (S // 128) * 512])
        u_sb = SBa("u_sb", [128, (E // 128) * 512])
        xn_sl = SBa("xn_sl", [128, NDC * 512])
        vring = SBa("vring", [128, 2 * S])
        wu = SBa("wu", [128, 2 * NDC * 128])
        wout = SBa("wout", [128, 16 * 128])
        buv_u = SBa("buv_u", [1, E])
        bout_sb = SBa("bout_sb", [1, D])
        a5_sb = SBa("a5_sb", [128, 2 * 512])
        psU = PSa("psU", [128, 2 * 512])
        psS = PSa("psS", [128, 2 * 512])
        psV = PSa("psV", [128, 2 * 512])
        psH = PSa("psH", [128, 2 * 512])

        s_ab2 = stream_sem()
        ld.emitm(s_ab2, lambda e: e.dma_start(out=buv_u[:], in_=b_uv[:, 0:E]))
        bo_tok = ld.emitm(s_ab2, lambda e: e.dma_start(out=bout_sb[:], in_=b_out[:, :]))
        bu_tok = bo_tok

        s_xsl = stream_sem()
        swu = [stream_sem(), stream_sem()]
        swv2 = [stream_sem(), stream_sem()]
        swo = stream_sem()
        wu_free = [None, None]
        v_free = [None, None]
        wo_free = None
        uev_prev = [None, None]
        rl_prev = [None, None]
        uav_prev = [None, None]
        a5_prev = [None, None]
        sq_toks = [None] * (S // 128)

        for qb in range(TOWN // 512):
            ld.wait(xn_store_all)
            ld.wait(wu_free[0], wu_free[1])
            xsl_tok = ld.emitm(s_xsl, lambda e, qb=qb: e.dma_start(
                out=xn_sl[:], in_=xn_hbm.rearrange("(dc p) t -> p dc t", p=128)[:, :, qb * 512:(qb + 1) * 512]))
            for fc in range(E // 128):
                wsl = fc % 2
                if wu_free[wsl]:
                    ld.wait(wu_free[wsl])
                wu_tok = ld.emitm(swu[wsl], lambda e, fc=fc, wsl=wsl: e.dma_start(
                    out=wu[:, wsl * NDC * 128:(wsl + 1) * NDC * 128],
                    in_=w_uv.rearrange("(dc p) f -> p dc f", p=128)[:, :, fc * 128:(fc + 1) * 128]))
                pairs = [(wu[:, wsl * NDC * 128 + dc * 128: wsl * NDC * 128 + (dc + 1) * 128],
                          xn_sl[:, dc * 512:(dc + 1) * 512]) for dc in range(NDC)]
                mm = mm_group(psU[:, wsl * 512:(wsl + 1) * 512], pairs,
                              bias=(buv_u[:, fc * 128:(fc + 1) * 128], ones[:, :]),
                              gate=[wu_tok, xsl_tok, bu_tok, ones_tok], free=uev_prev[wsl])
                wu_free[wsl] = mm
                sc.wait(mm)
                uev = sc.emitm(s_sc, lambda e, fc=fc, wsl=wsl: e.activation(
                    u_sb[:, fc * 512:(fc + 1) * 512], psU[:, wsl * 512:(wsl + 1) * 512], AF.Silu))
                uev_prev[wsl] = uev
            u_tok = (s_sc.idx, s_sc.count)

            for kt in range(S // 128):
                sl = kt % 2
                mm = mm_group(psS[:, sl * 512:(sl + 1) * 512],
                              [(k_fm[:, kt * 128:(kt + 1) * 128], q_fm[:, qb * 512:(qb + 1) * 512])],
                              gate=[kq_tok[0], sq_toks[kt]], free=rl_prev[sl])
                sc.wait(mm)
                rl = sc.emitm(s_sc, lambda e, kt=kt, sl=sl: e.activation(
                    st_sb[:, kt * 512:(kt + 1) * 512], psS[:, sl * 512:(sl + 1) * 512], AF.Relu))
                rl_prev[sl] = rl
                ve.wait(rl)
                sq_toks[kt] = ve.emitm(s_ve, lambda e, kt=kt: e.tensor_mul(
                    st_sb[:, kt * 512:(kt + 1) * 512], st_sb[:, kt * 512:(kt + 1) * 512], st_sb[:, kt * 512:(kt + 1) * 512]))
            st_all = (s_ve.idx, s_ve.count)

            for ec in range(E // 128):
                vsl = ec % 2
                if v_free[vsl]:
                    ld.wait(v_free[vsl])
                ld.wait(v_store_all0, v_store_all1)
                for part in range(4):
                    vt = ld.emitm(swv2[vsl], lambda e, ec=ec, vsl=vsl, part=part: e.dma_start(
                        out=vring[:, vsl * S + part * 1024: vsl * S + (part + 1) * 1024],
                        in_=v_hbm.rearrange("(kt p) e -> p kt e", p=128)[:, part * 8:(part + 1) * 8, ec * 128:(ec + 1) * 128]))
                pairs = [(vring[:, vsl * S + kt * 128: vsl * S + (kt + 1) * 128],
                          st_sb[:, kt * 512:(kt + 1) * 512]) for kt in range(S // 128)]
                mm = mm_group(psV[:, vsl * 512:(vsl + 1) * 512], pairs,
                              gate=[vt, st_all], free=uav_prev[vsl])
                v_free[vsl] = mm
                ve.wait(mm, u_tok)
                uav = ve.emitm(s_ve, lambda e, ec=ec, vsl=vsl: e.tensor_mul(
                    u_sb[:, ec * 512:(ec + 1) * 512], psV[:, vsl * 512:(vsl + 1) * 512], u_sb[:, ec * 512:(ec + 1) * 512]))
                uav_prev[vsl] = uav
            uav_tok = (s_ve.idx, s_ve.count)

            for hb in range(NDC):
                sl = hb % 2
                if wo_free:
                    ld.wait(wo_free)
                wo_tok = ld.emitm(swo, lambda e, hb=hb: e.dma_start(
                    out=wout[:], in_=w_out.rearrange("(ec p) h -> p ec h", p=128)[:, :, hb * 128:(hb + 1) * 128]))
                pairs = [(wout[:, ec * 128:(ec + 1) * 128], u_sb[:, ec * 512:(ec + 1) * 512])
                         for ec in range(E // 128)]
                mm = mm_group(psH[:, sl * 512:(sl + 1) * 512], pairs,
                              bias=(bout_sb[:, hb * 128:(hb + 1) * 128], ones[:, :]),
                              gate=[wo_tok, uav_tok, bo_tok], free=a5_prev[sl])
                wo_free = mm
                ve.wait(mm)
                if a5_prev[sl]:
                    ve.wait(a5_prev[sl])
                ve.wait((s_sa5[sl].idx, s_sa5[sl].count) if s_sa5[sl].count else None)
                a5e = ve.emitm(s_ve, lambda e, sl=sl: e.tensor_copy(
                    a5_sb[:, sl * 512:(sl + 1) * 512], psH[:, sl * 512:(sl + 1) * 512]))
                a5_prev[sl] = a5e
                dstore(s_sa5[sl], [a5e], a5_hbm[hb * 128:(hb + 1) * 128, qb * 512:(qb + 1) * 512],
                       a5_sb[:, sl * 512:(sl + 1) * 512])

        if debug == "a5":
            gp.wait((s_sa5[0].idx, s_sa5[0].count), (s_sa5[1].idx, s_sa5[1].count))
            ld.wait((s_sa5[0].idx, s_sa5[0].count), (s_sa5[1].idx, s_sa5[1].count))
            bstate = None
            s_bn = stream_sem()
            for hb in range(NDC):
                for qb in range(TOWN // 512):
                    ld.wait(bstate)
                    lt = ld.emitm(s_bn, lambda e, hb=hb, qb=qb: e.dma_start(
                        out=a5_sb[:, 0:512], in_=a5_hbm[hb * 128:(hb + 1) * 128, qb * 512:(qb + 1) * 512]))
                    bstate = dstore(s_dbg, [lt], dbg[hb * 128:(hb + 1) * 128, qb * 512:(qb + 1) * 512], a5_sb[:, 0:512])
            return finish([s_dbg])

    phA.close()
    p.barrier()

    a5_store_all = (s_sa5[0].idx, s_sa5[0].count)
    a5_store_all1 = (s_sa5[1].idx, s_sa5[1].count)
    ret_debug = [False]
    with contextlib.ExitStack() as sco:
        SBb = lambda n, shp: sco.enter_context(nc.sbuf_tensor(n, list(shp), F32))
        xb = SBb("xb", [128, NDC * TB])
        xb_state = {"wr": None}

        with contextlib.ExitStack() as s2:
            SBx = lambda n, shp: s2.enter_context(nc.sbuf_tensor(n, list(shp), F32))
            x_t2 = SBx("x_t2", [128, 2 * D])
            xn_t2 = SBx("xn_t2", [128, 2 * D])
            ln_state["sq"] = SBx("ln_sq_b", [128, D])
            psT2 = s2.enter_context(nc.psum_tensor("psT2", [128, 1024], F32))
            psT_free = None
            sx2 = [stream_sem(), stream_sem()]
            x2_free = [None, None]
            for t in range(NT):
                sl = t % 2
                ld.wait(x2_free[sl])
                lt = ld.emitm(sx2[sl], lambda e, t=t, sl=sl: e.dma_start(
                    out=x_t2[:, sl * D:(sl + 1) * D], in_=x_bwin[t * 128:(t + 1) * 128, :]))
                lnt = ln_tile(x_t2[:, sl * D:(sl + 1) * D], xn_t2[:, sl * D:(sl + 1) * D], [lt])
                x2_free[sl] = lnt
                te.wait(lnt, i_tok)
                if psT_free:
                    te.wait(psT_free)
                for dc in range(NDC):
                    fn = lambda e, dc=dc, sl=sl: e.transpose(
                        psT2[:, dc * 128:(dc + 1) * 128], xn_t2[:, sl * D + dc * 128: sl * D + (dc + 1) * 128], ident[:])
                    tt = te.emitm(s_te, fn) if dc == NDC - 1 else te.emit(fn)
                ve.wait(tt)
                ev = ve.emitm(s_ve, lambda e, t=t: e.tensor_copy(
                    fm_ap(xb, TB, t * 128, 128), psT2[:].rearrange("p (dc c) -> p dc c", c=128)))
                psT_free = ev
        xb_tok = ev

        if debug == "xb0":
            dstore(s_dbg, [xb_tok], dbg[:, :], xb[:])
            return finish([s_dbg])

        def do_layer(l):
            QT, AT, FT = QKV_T[l], ATTN_T[l], FFN_T[l]
            p.barrier()
            with contextlib.ExitStack() as s2:
                SBl = lambda n, shp: s2.enter_context(nc.sbuf_tensor(f"{n}_l{l}", list(shp), F32))
                PSl = lambda n, shp: s2.enter_context(nc.psum_tensor(f"{n}_l{l}", list(shp), F32))
                GRP = 2
                NSLOT = 6
                invf = SBl("invf", [1, DH])
                lbase = SBl("lbase", [1, DH])
                coef = SBl("coef", [128, 4 * DH])
                rowbuf = SBl("rowbuf", [1, 2 * DH])
                tcol = SBl("tcol", [128, GRP])
                pwc = SBl("pwc", [128, GRP])
                invf_b = SBl("invf_b", [128, DH])
                lbase_b = SBl("lbase_b", [128, DH])
                frq = SBl("frq", [128, GRP * DH])
                kst_t = SBl("kst_t", [128, GRP * DH])
                kct_t = SBl("kct_t", [128, GRP * DH])
                sarg = SBl("sarg", [128, GRP * DH])
                carg = SBl("carg", [128, GRP * DH])
                sins = SBl("sins", [128, GRP * DH])
                coss = SBl("coss", [128, GRP * DH])
                scl = SBl("scl", [128, GRP * DH])
                scli = SBl("scli", [128, GRP * DH])
                tabs = SBl("tabs", [128, 4 * GRP * DH])
                wq3 = SBl("wq3", [128, NDC * 256])
                qkv_t = SBl("qkv_t", [128, GRP * 3 * INNER])
                sqv = SBl("sqv", [128, 2 * INNER])
                nrm = SBl("nrm", [128, 16])
                qn_sb = SBl("qn_sb", [128, 2 * INNER])
                qhat = SBl("qhat", [128, 2 * INNER])
                qf = SBl("qf", [128, 4 * 128])
                khr = SBl("khr", [128, 4 * NSLOT * 128])
                vr = SBl("vr", [128, NSLOT * INNER])
                msk = SBl("msk", [128, NKB])
                s_sb = SBl("s_sb", [128, NKB])
                p_sb = SBl("p_sb", [128, NKB])
                pt_sb = SBl("pt_sb", [128, NKB])
                sums = SBl("sums", [128, 1])
                nmax = SBl("nmax", [128, 1])
                rcp = SBl("rcp", [128, 1])
                og = SBl("og", [128, GRP * INNER])
                ogf = SBl("ogf", [128, 4 * GRP * 128])
                wo_sl = SBl("wo_sl", [128, 512])
                ps_q = PSl("ps_q", [128, 512])
                ps_s = PSl("ps_s", [128, NKB])
                ps_p = PSl("ps_p", [128, NKB])
                ps_o = PSl("ps_o", [128, 2 * DH])
                ps_w = PSl("ps_w", [128, 512])
                ps_t64 = PSl("ps_t64", [128, 64])

                st = {"tab_tok": None, "ps_q_free": None, "wq_free": None,
                      "sqv_free": None, "qn_free": None, "qhat_free": None,
                      "qf_free": None, "ps_s_free": None, "ps_p_free": None,
                      "ps_o_free": [None, None], "ps_w_free": None,
                      "msk_free": None, "ogf_free": None, "wo_free": [None, None],
                      "s_sb_free": None, "p_sb_free": None, "stat_free": None,
                      "ring_free": {}, "o_written": {}, "ex_prev": None,
                      "rc_prev": None, "sm_prev": None}

                wqs = stream_sem()
                s_ktab, s_mask = stream_sem(), stream_sem()
                wos = [stream_sem(), stream_sem()]
                s_row = stream_sem()

                cts = None
                for j, src in enumerate([qs_row, qs_sh, ks_row, ks_sh]):
                    ld.wait(cts)
                    rt = ld.emitm(s_row, lambda e, src=src: e.dma_start(out=rowbuf[:, DH:2 * DH], in_=src[l:l + 1, :]))
                    mmb = mm_one(ps_t64[:, 0:DH], ones[:, 0:128], rowbuf[:, DH:2 * DH], True, True,
                                 gate=[rt, ones_tok], free=cts)
                    ve.wait(mmb)
                    cts = ve.emitm(s_ve, lambda e, j=j: e.tensor_copy(coef[:, j * DH:(j + 1) * DH], ps_t64[:, 0:DH]))
                ld.wait(cts)
                ep_tok = ld.emitm(s_row, lambda e: e.dma_start(out=rowbuf[:, DH:2 * DH], in_=eps_row[:, :]))
                epb = mm_one(ps_t64[:, 0:DH], ones[:, 0:128], rowbuf[:, DH:2 * DH], True, True,
                             gate=[ep_tok, ones_tok], free=cts)
                ve.wait(epb)
                e1 = ve.emitm(s_ve, lambda e: e.tensor_mul(coef[:, DH:2 * DH], coef[:, DH:2 * DH], ps_t64[:, 0:DH]))
                e2 = ve.emitm(s_ve, lambda e: e.tensor_mul(coef[:, 3 * DH:4 * DH], coef[:, 3 * DH:4 * DH], ps_t64[:, 0:DH]))
                ve.wait(e1, e2)
                e3 = ve.emitm(s_ve, lambda e: e.tensor_scalar_mul(coef[:, 0:DH], coef[:, 0:DH], 8.0))
                e4 = ve.emitm(s_ve, lambda e: e.tensor_scalar_mul(coef[:, DH:2 * DH], coef[:, DH:2 * DH], 8.0))
                coef_tok = e4

                def build_tables(g0, ntile):
                    ld.wait(st.get("ktab_free"))
                    lt1 = ld.emitm(s_ktab, lambda e, g0=g0, ntile=ntile: e.dma_start(
                        out=coss[:, 0:ntile * DH],
                        in_=cos_h.rearrange("(t p) c -> p t c", p=128)[:, g0:g0 + ntile, :]))
                    lt2 = ld.emitm(s_ktab, lambda e, g0=g0, ntile=ntile: e.dma_start(
                        out=sins[:, 0:ntile * DH],
                        in_=sin_h.rearrange("(t p) c -> p t c", p=128)[:, g0:g0 + ntile, :]))
                    lt3 = ld.emitm(s_ktab, lambda e, g0=g0, ntile=ntile: e.dma_start(
                        out=scl[:, 0:ntile * DH],
                        in_=scl_h.rearrange("(t p) c -> p t c", p=128)[:, g0:g0 + ntile, :]))
                    ve.wait(lt3, st["tab_tok"])
                    rc1 = ve.emitm(s_ve, lambda e, ntile=ntile: e.reciprocal(scli[:, 0:ntile * DH], scl[:, 0:ntile * DH]))
                    for i in range(ntile):
                        ve.wait(rc1, coef_tok)
                        tA = []
                        for dst, raw, sclx in ((0, coss, scl), (1, sins, scl), (2, coss, scli), (3, sins, scli)):
                            tA.append(ve.emitm(s_ve, lambda e, dst=dst, raw=raw, sclx=sclx, i=i: e.tensor_mul(
                                tabs[:, (dst * GRP + i) * DH:(dst * GRP + i + 1) * DH], raw[:, i * DH:(i + 1) * DH], sclx[:, i * DH:(i + 1) * DH])))
                        ve.wait(*tA)
                        tB = []
                        for dst in range(4):
                            tB.append(ve.emitm(s_ve, lambda e, dst=dst, i=i: e.tensor_mul(
                                tabs[:, (dst * GRP + i) * DH:(dst * GRP + i + 1) * DH],
                                tabs[:, (dst * GRP + i) * DH:(dst * GRP + i + 1) * DH], coef[:, dst * DH:(dst + 1) * DH])))
                        st["tab_tok"] = tB[-1]
                    st["ktab_free"] = (s_ve.idx, s_ve.count)
                    return (s_ve.idx, s_ve.count)

                def qkv_group(g0, ntile):
                    for ch in range(6):
                        ld.wait(st["wq_free"])
                        wt = ld.emitm(wqs, lambda e, ch=ch: e.dma_start(
                            out=wq3[:], in_=wqkv_h.rearrange("l (dc p) f -> l p dc f", p=128)[l][:, :, ch * 256:(ch + 1) * 256]))
                        for i in range(ntile):
                            t = g0 + i
                            pairs = []
                            for dc in range(NDC):
                                off = dc * TB + t * 128
                                lhs = bass.AP(tensor=xb.ap().tensor, offset=off if dc < 4 else off - 1,
                                              ap=[[NDC * TB, 128], [1, 128]])
                                pairs.append((lhs, wq3[:, dc * 256:(dc + 1) * 256]))
                            mm = mm_group(ps_q[:, 0:256], pairs,
                                          gate=[wt, xb_tok, xb_state["wr"]], free=st["ps_q_free"])
                            ve.wait(mm)
                            cp = ve.emitm(s_ve, lambda e, i=i, ch=ch: e.tensor_copy(
                                qkv_t[:, i * 3 * INNER + ch * 256: i * 3 * INNER + (ch + 1) * 256],
                                ps_q[:, 0:256]))
                            st["ps_q_free"] = cp
                        st["wq_free"] = mm
                    return (s_ve.idx, s_ve.count)

                def prep_tile(i, t, qkv_cp):
                    base = i * 3 * INNER
                    qk = qkv_t[:, base: base + 2 * INNER]
                    sc.wait(qkv_cp, st["sqv_free"])
                    sq1 = sc.emitm(s_sc, lambda e, qk=qk: e.activation(sqv[:], qk, AF.Square))
                    ve.wait(sq1)
                    n1 = ve.emitm(s_ve, lambda e: e.reduce_sum(nrm[:, :].unsqueeze(2), sqv[:].rearrange("p (s c) -> p s c", c=DH), axis=AX.X))
                    sc.wait(n1)
                    n2 = sc.emitm(s_sc, lambda e: e.activation(nrm[:], nrm[:], AF.Sqrt))
                    ve.wait(n2)
                    n3 = ve.emitm(s_ve, lambda e: e.tensor_scalar_max(nrm[:], nrm[:], 1e-12))
                    ve.wait(n3)
                    n4 = ve.emitm(s_ve, lambda e: e.reciprocal(nrm[:], nrm[:]))
                    ve.wait(n4, st["qn_free"])
                    rb = nrm[:, :].unsqueeze(2).broadcast_to((128, 16, DH))
                    qn = ve.emitm(s_ve, lambda e, qk=qk, rb=rb: e.tensor_tensor(
                        qn_sb[:].rearrange("p (s c) -> p s c", c=DH), qk.rearrange("p (s c) -> p s c", c=DH), rb, ALU.mult))
                    ve.wait(qn, st["tab_tok"], st["qhat_free"])
                    outs = []
                    for half, (Aj, Bj) in ((0, (0, 1)), (1, (2, 3))):
                        A_b = bass.AP(tensor=tabs.ap().tensor, offset=(Aj * GRP + i) * DH,
                                      ap=[[4 * GRP * DH, 128], [0, HEADS], [1, DH]])
                        B_b = bass.AP(tensor=tabs.ap().tensor, offset=(Bj * GRP + i) * DH,
                                      ap=[[4 * GRP * DH, 128], [0, HEADS], [32, 2], [1, 32]])
                        seg = qn_sb[:, half * INNER:(half + 1) * INNER]
                        h1 = ve.emitm(s_ve, lambda e, seg=seg, A_b=A_b, half=half: e.tensor_tensor(
                            qhat[:, half * INNER:(half + 1) * INNER].rearrange("p (s c) -> p s c", c=DH),
                            seg.rearrange("p (s c) -> p s c", c=DH), A_b, ALU.mult))
                        swap = bass.AP(tensor=qn_sb.ap().tensor, offset=half * INNER + 32,
                                       ap=[[2 * INNER, 128], [DH, HEADS], [-32, 2], [1, 32]])
                        h2 = ve.emitm(s_ve, lambda e, swap=swap, B_b=B_b, half=half: e.tensor_tensor(
                            sqv[:, half * INNER:(half + 1) * INNER].rearrange("p (s h c) -> p s h c", h=2, c=32),
                            swap, B_b, ALU.mult))
                        ve.wait(h1, h2)
                        outs.append(ve.emitm(s_ve, lambda e, half=half: e.tensor_add(
                            qhat[:, half * INNER:(half + 1) * INNER], qhat[:, half * INNER:(half + 1) * INNER], sqv[:, half * INNER:(half + 1) * INNER])))
                    st["qn_free"] = outs[-1]
                    st["sqv_free"] = outs[-1]
                    st["qhat_free"] = None
                    slot = t % NSLOT
                    te.wait(outs[0], outs[1], i_tok, st["qf_free"], st["ring_free"].get(slot), st["ps_p_free"])
                    for fc in range(4):
                        fn = lambda e, fc=fc: e.transpose(ps_p[:, fc * 128:(fc + 1) * 128], qhat[:, fc * 128:(fc + 1) * 128], ident[:])
                        tq = te.emitm(s_te, fn) if fc == 3 else te.emit(fn)
                    ve.wait(tq)
                    c1 = ve.emitm(s_ve, lambda e: e.tensor_copy(qf[:], ps_p[:, 0:512]))
                    te.wait(c1)
                    for fc in range(4):
                        fn = lambda e, fc=fc: e.transpose(ps_p[:, fc * 128:(fc + 1) * 128], qhat[:, INNER + fc * 128: INNER + (fc + 1) * 128], ident[:])
                        tk = te.emitm(s_te, fn) if fc == 3 else te.emit(fn)
                    st["qhat_free"] = tk
                    ve.wait(tk)
                    c2 = ve.emitm(s_ve, lambda e, slot=slot: e.tensor_copy(
                        bass.AP(tensor=khr.ap().tensor, offset=slot * 128,
                                ap=[[4 * NSLOT * 128, 128], [NSLOT * 128, 4], [1, 128]]),
                        ps_p[:, 0:512].rearrange("p (fc c) -> p fc c", c=128)))
                    st["ps_p_free"] = c2
                    v_cp = ve.emitm(s_ve, lambda e, i=i, slot=slot: e.tensor_copy(
                        vr[:, slot * INNER:(slot + 1) * INNER], qkv_t[:, i * 3 * INNER + 2 * INNER: (i + 1) * 3 * INNER]))
                    return c1

                def attn_tile(t, c1):
                    mi = t - AT
                    ld.wait(st["msk_free"])
                    mt = ld.emitm(s_mask, lambda e, mi=mi: e.dma_start(out=msk[:], in_=masks_h[l][mi]))
                    ring_all = (s_ve.idx, s_ve.count)
                    slots = [(t - 4 + j) % NSLOT for j in range(BAND)]
                    sm = None
                    for h in range(HEADS):
                        fc, plo = h // 2, (h % 2) * DH
                        te.wait(c1, ring_all, st["ps_s_free"])
                        for j, sl_ in enumerate(slots):
                            la = qf[plo:plo + DH, fc * 128:(fc + 1) * 128]
                            ra = khr[plo:plo + DH, fc * NSLOT * 128 + sl_ * 128: fc * NSLOT * 128 + (sl_ + 1) * 128]
                            fn = lambda e, la=la, ra=ra, j=j: e.matmul(
                                ps_s[:, j * 128:(j + 1) * 128], la, ra, start=True, stop=True)
                            mmS = te.emitm(s_te, fn) if j == BAND - 1 else te.emit(fn)
                        ve.wait(mmS, mt, st["ex_prev"])
                        sm = ve.emitm(s_ve, lambda e: e.tensor_add(s_sb[:], ps_s[:], msk[:]))
                        st["ps_s_free"] = sm
                        ve.wait(sm, st["ex_prev"])
                        rm = ve.emitm(s_ve, lambda e: e.reduce_max(nmax[:], s_sb[:], axis=AX.X, negate=True))
                        sc.wait(rm, st["rc_prev"], st["p_sb_free"])
                        ex = sc.emitm(s_sc, lambda e: e.activation(p_sb[:], s_sb[:], AF.Exp, bias=nmax[:, 0:1], accum_out=sums[:]))
                        st["ex_prev"] = ex
                        ve.wait(ex)
                        rc = ve.emitm(s_ve, lambda e: e.reciprocal(rcp[:], sums[:]))
                        st["rc_prev"] = rc
                        te.wait(ex, st["ps_p_free"])
                        for j in range(BAND):
                            fn = lambda e, j=j: e.transpose(ps_p[:, j * 128:(j + 1) * 128], p_sb[:, j * 128:(j + 1) * 128], ident[:])
                            tp = te.emitm(s_te, fn) if j == BAND - 1 else te.emit(fn)
                        st["p_sb_free"] = tp
                        ve.wait(tp)
                        ptc = ve.emitm(s_ve, lambda e: e.tensor_copy(pt_sb[:], ps_p[:]))
                        st["ps_p_free"] = ptc
                        psl = h % 2
                        pairs = [(pt_sb[:, j * 128:(j + 1) * 128],
                                  vr[:, slots[j] * INNER + h * DH: slots[j] * INNER + (h + 1) * DH])
                                 for j in range(BAND)]
                        mmO = mm_group(ps_o[:, psl * DH:(psl + 1) * DH], pairs, gate=[ptc, ring_all],
                                       free=st["ps_o_free"][psl])
                        ve.wait(mmO, rc, st.get("og_free"))
                        oev = ve.emitm(s_ve, lambda e, t=t, h=h, psl=psl: e.tensor_scalar(
                            og[:, (t % GRP) * INNER + h * DH: (t % GRP) * INNER + (h + 1) * DH],
                            ps_o[:, psl * DH:(psl + 1) * DH], rcp[:, 0:1], None, ALU.mult))
                        st["ps_o_free"][psl] = oev
                    st["msk_free"] = sm
                    st["qf_free"] = (s_te.idx, s_te.count)
                    st["o_written"][t % GRP] = oev
                    st["ring_free"][(t + 2) % NSLOT] = (s_te.idx, s_te.count)
                    return oev

                def wo_group(g0, ntile, qkv_guard):
                    for i in range(ntile):
                        slot = (g0 + i) % GRP
                        te.wait(i_tok, st["ogf_free"], st["o_written"].get(slot), st["ps_p_free"])
                        for fc in range(4):
                            fn = lambda e, slot=slot, fc=fc: e.transpose(
                                ps_p[:, fc * 128:(fc + 1) * 128], og[:, slot * INNER + fc * 128: slot * INNER + (fc + 1) * 128], ident[:])
                            to_ = te.emitm(s_te, fn) if fc == 3 else te.emit(fn)
                        ve.wait(to_)
                        cp = ve.emitm(s_ve, lambda e, i=i: e.tensor_copy(
                            bass.AP(tensor=ogf.ap().tensor, offset=i * 128,
                                    ap=[[4 * GRP * 128, 128], [GRP * 128, 4], [1, 128]]),
                            ps_p[:, 0:512].rearrange("p (fc c) -> p fc c", c=128)))
                        st["ps_p_free"] = cp
                    st["og_free"] = (s_te.idx, s_te.count)
                    ogf_tok = (s_ve.idx, s_ve.count)
                    W = ntile * 128
                    wb = None
                    for dc in range(NDC):
                        ld.wait(st["wo_free"][0])
                        wt = ld.emitm(wos[0], lambda e, dc=dc: e.dma_start(
                            out=wo_sl[:],
                            in_=wo_h.rearrange("l (ic p) d -> l p ic d", p=128)[l][:, :, dc * 128:(dc + 1) * 128]))
                        pairs = [(wo_sl[:, ic * 128:(ic + 1) * 128],
                                  ogf[:, ic * GRP * 128: ic * GRP * 128 + W]) for ic in range(4)]
                        mm = mm_group(ps_w[:, 0:W], pairs, gate=[wt, ogf_tok, qkv_guard], free=st["ps_w_free"])
                        st["wo_free"][0] = mm
                        ve.wait(mm)
                        xs = xb[:, dc * TB + g0 * 128: dc * TB + g0 * 128 + W]
                        wb = ve.emitm(s_ve, lambda e, xs=xs, W=W: e.scalar_tensor_tensor(
                            xs, ps_w[:, 0:W], 1.0, xs, ALU.mult, ALU.add))
                        st["ps_w_free"] = wb
                    st["ogf_free"] = mm
                    xb_state["wr"] = wb
                    return wb

                if debug == "as0" and l == 0:
                    dstore(s_dbg, [coef_tok], dbg[:, :], coef[:])
                    ret_debug[0] = True
                    return
                ntile_all = NT - QT
                groups = [(QT + GRP * g, min(GRP, NT - (QT + GRP * g))) for g in range((ntile_all + GRP - 1) // GRP)]
                pend = None
                for (g0, ntile) in groups:
                    r_ = build_tables(g0, ntile)
                    if r_ == "STOP":
                        dstore(s_dbg, st["_stop"], dbg[:, :], sarg[:, 0:128])
                        ret_debug[0] = True
                        return
                    if debug == "as1" and l == 0:
                        dstore(s_dbg, [st["tab_tok"]], dbg[:, :], tabs[:])
                        ret_debug[0] = True
                        return
                    qkv_cp = qkv_group(g0, ntile)
                    if debug == "as2" and l == 0:
                        dstore(s_dbg, [qkv_cp], dbg[:, :], qkv_t[:, 0:3 * INNER])
                        ret_debug[0] = True
                        return
                    if pend:
                        wo_group(pend[0], pend[1], qkv_cp)
                        pend = None
                    a_start = None
                    for i in range(ntile):
                        t = g0 + i
                        c1 = prep_tile(i, t, qkv_cp)
                        if debug == "as3" and l == 0:
                            dstore(s_dbg, [c1, (kernel_sve := None) or (s_ve.idx, s_ve.count)], dbg[:, :], qhat[:])
                            ret_debug[0] = True
                            return
                        if t >= AT:
                            if a_start is None:
                                a_start = t
                            attn_tile(t, c1)
                            if debug == "as4" and l == 0:
                                dstore(s_dbg, [(s_ve.idx, s_ve.count)], dbg[:, :], p_sb[:])
                                ret_debug[0] = True
                                return
                    if a_start is not None:
                        pend = (a_start, g0 + ntile - a_start)
                    if debug == "as5" and l == 0 and pend and pend[0] is not None and g0 >= AT:
                        wo_group(pend[0], pend[1], None)
                        dstore(s_dbg, [xb_state["wr"]], dbg[:, 0:512], xb[:, 0:512])
                        ret_debug[0] = True
                        return
                if pend:
                    wo_group(pend[0], pend[1], None)
                    pend = None
                if l == 1:
                    bc = OWN0 * 128 - 1
                    kap = bass.AP(tensor=xb.ap().tensor, offset=bc, ap=[[NDC * TB, 128], [TB, NDC], [1, 1]])
                    ve.wait(xb_state["wr"], const_tok)
                    xb_state["wr"] = ve.emitm(s_ve, lambda e, kap=kap: e.tensor_scalar(
                        kap, kap, kill_sb[:, 0:1], None, ALU.mult))

            if debug == "xmid0" and l == 0:
                dstore(s_dbg, [xb_state["wr"]], dbg[:, :], xb[:])
                ret_debug[0] = True
                return

            p.barrier()
            with contextlib.ExitStack() as s3:
                SBf = lambda n, shp: s3.enter_context(nc.sbuf_tensor(f"{n}_f{l}", list(shp), F32))
                PSf = lambda n, shp: s3.enter_context(nc.psum_tensor(f"{n}_f{l}", list(shp), F32))
                h2n = SBf("h2n", [128, NDC * 512])
                h4 = SBf("h4", [128, NHC * 512])
                sqf = SBf("sqf", [128, 512])
                gel = sqf
                stat = SBf("stat", [1, 2 * 512])
                nmb = SBf("nmb", [128, 512])
                rsb = SBf("rsb", [128, 512])
                w1a = SBf("w1a", [128, NDC * 128])
                w1g = SBf("w1g", [128, NDC * 128])
                w2t = SBf("w2t", [128, NHC * 128])
                b1s = SBf("b1s", [1, 2 * 128])
                b2s = SBf("b2s", [1, 2 * 128])
                ps_st = PSf("ps_st", [1, 2 * 512])
                ps_bc = PSf("ps_bc", [128, 2 * 512])
                ps_a = PSf("ps_a", [128, 512])
                ps_g = PSf("ps_g", [128, 512])
                ps_d = PSf("ps_d", [128, 2 * 512])

                w1s, w2s, b1st = stream_sem(), stream_sem(), stream_sem()
                b2st = stream_sem()
                if debug == "ffn0blk":
                    ve.emitm(s_ve, lambda e: e.memset(h4[:], 0.0))
                fs = {"stat_free": None, "sq_free": None, "bc_free": None,
                      "nmb_free": None, "h2n_free": None, "a_free": None,
                      "g_free": None, "gel_free": None, "h4_free": {},
                      "d_free": [None, None], "w1_free": None, "w2_free": None,
                      "b1_free": None, "w2_mm": None}

                blocks = []
                off = FT * 128
                while off < NT * 128:
                    w = min(512, NT * 128 - off)
                    blocks.append((off, w))
                    off += w

                for (boff, W) in reversed(blocks):
                    def h2ap(dc, boff=boff, W=W):
                        off2 = dc * TB + boff
                        return bass.AP(tensor=xb.ap().tensor, offset=off2 if dc < 4 else off2 - 1,
                                       ap=[[NDC * TB, 128], [1, W]])
                    for dc in range(NDC):
                        mms = mm_one(ps_st[0:1, 0:W], ones_col[:, 0:1], h2ap(dc), dc == 0, dc == NDC - 1,
                                     gate=[oc_tok, xb_tok, xb_state["wr"]],
                                     free=fs["stat_free"] if dc == 0 else None, mark=dc == NDC - 1)
                    for dc in range(NDC):
                        sc.wait(fs["sq_free"], fs["gel_free"], xb_state["wr"])
                        sq = sc.emitm(s_sc, lambda e, dc=dc, W=W, h2ap=h2ap: e.activation(sqf[:, 0:W], h2ap(dc), AF.Square))
                        mmq = mm_one(ps_st[0:1, 512:512 + W], ones_col[:, 0:1], sqf[:, 0:W], dc == 0, dc == NDC - 1,
                                     gate=[sq, oc_tok], free=fs["stat_free"] if dc == 0 else None,
                                     mark=True)
                        fs["sq_free"] = mmq
                    ve.wait(mms, mmq, fs["bc_free"])
                    f1 = ve.emitm(s_ve, lambda e, W=W: e.tensor_scalar_mul(stat[0:1, 0:W], ps_st[0:1, 0:W], -1.0 / D))
                    f2 = ve.emitm(s_ve, lambda e, W=W: e.tensor_scalar_mul(stat[0:1, 512:512 + W], ps_st[0:1, 512:512 + W], 1.0 / D))
                    fs["stat_free"] = f2
                    ve.wait(f1, f2)
                    f3 = ve.emitm(s_ve, lambda e, W=W: e.scalar_tensor_tensor(
                        stat[0:1, 512:512 + W], stat[0:1, 0:W], -1.0, stat[0:1, 512:512 + W], ALU.mult, ALU.add))
                    ve.wait(f3)
                    f2b = ve.emitm(s_ve, lambda e, W=W: e.tensor_scalar_mul(stat[0:1, 512:512 + W], ps_st[0:1, 512:512 + W], 1.0 / D))
                    ve.wait(f2b)
                    f4 = ve.emitm(s_ve, lambda e, W=W: e.tensor_mul(sqf[0:1, 0:W], stat[0:1, 0:W], stat[0:1, 0:W]))
                    ve.wait(f4)
                    f5 = ve.emitm(s_ve, lambda e, W=W: e.tensor_sub(stat[0:1, 512:512 + W], stat[0:1, 512:512 + W], sqf[0:1, 0:W]))
                    ve.wait(f5)
                    f6 = ve.emitm(s_ve, lambda e, W=W: e.tensor_scalar_add(stat[0:1, 512:512 + W], stat[0:1, 512:512 + W], EPS_LN))
                    sc.wait(f6)
                    f7 = sc.emitm(s_sc, lambda e, W=W: e.activation(stat[0:1, 512:512 + W], stat[0:1, 512:512 + W], AF.Sqrt))
                    ve.wait(f7)
                    f8 = ve.emitm(s_ve, lambda e, W=W: e.reciprocal(stat[0:1, 512:512 + W], stat[0:1, 512:512 + W]))
                    m1 = mm_one(ps_bc[:, 0:W], ones[:, 0:128], stat[0:1, 0:W], True, True,
                                gate=[f8, ones_tok], free=fs["nmb_free"])
                    m2 = mm_one(ps_bc[:, 512:512 + W], ones[:, 0:128], stat[0:1, 512:512 + W], True, True,
                                gate=[f8, ones_tok], free=fs["nmb_free"])
                    ve.wait(m1, m2)
                    g1 = ve.emitm(s_ve, lambda e, W=W: e.tensor_copy(nmb[:, 0:W], ps_bc[:, 0:W]))
                    g2 = ve.emitm(s_ve, lambda e, W=W: e.tensor_copy(rsb[:, 0:W], ps_bc[:, 512:512 + W]))
                    fs["bc_free"] = g2
                    h2n_toks = []
                    for dc in range(NDC):
                        ve.wait(g1, g2, fs["h2n_free"])
                        q1 = ve.emitm(s_ve, lambda e, dc=dc, W=W, h2ap=h2ap: e.tensor_add(h2n[:, dc * 512: dc * 512 + W], h2ap(dc), nmb[:, 0:W]))
                        ve.wait(q1)
                        q2 = ve.emitm(s_ve, lambda e, dc=dc, W=W: e.tensor_mul(h2n[:, dc * 512: dc * 512 + W], h2n[:, dc * 512: dc * 512 + W], rsb[:, 0:W]))
                        ve.wait(q2, lng_tok)
                        q3 = ve.emitm(s_ve, lambda e, dc=dc, W=W: e.tensor_scalar(
                            h2n[:, dc * 512: dc * 512 + W], h2n[:, dc * 512: dc * 512 + W],
                            lng[:, l * 16 + dc * 2: l * 16 + dc * 2 + 1], lng[:, l * 16 + dc * 2 + 1: l * 16 + dc * 2 + 2],
                            ALU.mult, ALU.add))
                        h2n_toks.append(q3)
                    h2n_tok = (s_ve.idx, s_ve.count)
                    fs["nmb_free"] = h2n_tok

                    for c in range(NHC):
                        cw = min(128, HID - c * 128)
                        ld.wait(fs["w1_free"])
                        wt1 = ld.emitm(w1s, lambda e, c=c, cw=cw: e.dma_start(
                            out=w1a[:, 0:NDC * cw],
                            in_=w1_h.rearrange("l (dc p) f -> l p dc f", p=128)[l][:, :, c * 128: c * 128 + cw]))
                        wt2 = ld.emitm(w1s, lambda e, c=c, cw=cw: e.dma_start(
                            out=w1g[:, 0:NDC * cw],
                            in_=w1_h.rearrange("l (dc p) f -> l p dc f", p=128)[l][:, :, HID + c * 128: HID + c * 128 + cw]))
                        ld.wait(fs["b1_free"])
                        bt1 = ld.emitm(b1st, lambda e, c=c, cw=cw: e.dma_start(
                            out=b1s[:, 0:cw], in_=b1_h[l:l + 1, c * 128: c * 128 + cw]))
                        bt2 = ld.emitm(b1st, lambda e, c=c, cw=cw: e.dma_start(
                            out=b1s[:, 128:128 + cw], in_=b1_h[l:l + 1, HID + c * 128: HID + c * 128 + cw]))
                        pa = [(w1a[:, dc * cw:(dc + 1) * cw], h2n[:, dc * 512: dc * 512 + W]) for dc in range(NDC)]
                        mma = mm_group(ps_a[0:cw, 0:W], pa, bias=(b1s[:, 0:cw], ones[:, 0:W]),
                                       gate=[wt2, bt2, h2n_tok, ones_tok], free=fs["a_free"])
                        pg = [(w1g[:, dc * cw:(dc + 1) * cw], h2n[:, dc * 512: dc * 512 + W]) for dc in range(NDC)]
                        mmg = mm_group(ps_g[0:cw, 0:W], pg, bias=(b1s[:, 128:128 + cw], ones[:, 0:W]),
                                       gate=[wt2, bt2, h2n_tok, ones_tok], free=fs["g_free"])
                        fs["w1_free"] = mmg
                        fs["b1_free"] = mmg
                        sc.wait(mmg, fs["gel_free"], fs["sq_free"])
                        ge = sc.emitm(s_sc, lambda e, cw=cw, W=W: e.activation(gel[0:cw, 0:W], ps_g[0:cw, 0:W], AF.Gelu))
                        fs["g_free"] = ge
                        fs["sq_free"] = ge
                        ve.wait(ge, mma, fs["h4_free"].get(c))
                        hv = ve.emitm(s_ve, lambda e, c=c, cw=cw, W=W: e.tensor_tensor(
                            h4[0:cw, c * 512: c * 512 + W], ps_a[0:cw, 0:W], gel[0:cw, 0:W], ALU.mult))
                        fs["a_free"] = hv
                        fs["gel_free"] = hv
                        fs["h4_free"][c] = None
                    h4_tok = (s_ve.idx, s_ve.count)

                    for dc in range(NDC):
                        sl = dc % 2
                        ld.wait(fs["w2_free"])
                        wt3 = ld.emitm(w2s, lambda e, dc=dc: e.dma_start(
                            out=w2t[:, 0:21 * 128].rearrange("p (c d) -> p c d", d=128),
                            in_=w2_h[l, 0:21 * 128, dc * 128:(dc + 1) * 128].rearrange("(c p) d -> p c d", p=128)))
                        wt4 = ld.emitm(w2s, lambda e, dc=dc: e.dma_start(
                            out=w2t[0:42, 21 * 128:22 * 128],
                            in_=w2_h[l, 21 * 128:HID, dc * 128:(dc + 1) * 128]))
                        ld.wait(fs.get("b2_free"))
                        b2t = ld.emitm(b2st, lambda e, dc=dc, sl=sl: e.dma_start(
                            out=b2s[:, sl * 128:(sl + 1) * 128], in_=b2_h[l:l + 1, dc * 128:(dc + 1) * 128]))
                        te.wait(wt4, h4_tok, b2t, fs["d_free"][sl])
                        te.emit(lambda e, dc=dc, sl=sl, W=W: e.matmul(
                            ps_d[:, sl * 512: sl * 512 + W], b2s[:, sl * 128:(sl + 1) * 128], ones[:, 0:W], start=True, stop=False))
                        for c in range(NHC):
                            cw = min(128, HID - c * 128)
                            fn = lambda e, c=c, cw=cw, dc=dc, sl=sl, W=W: e.matmul(
                                ps_d[:, sl * 512: sl * 512 + W], w2t[0:cw, c * 128:(c + 1) * 128],
                                h4[0:cw, c * 512: c * 512 + W], start=False, stop=c == NHC - 1)
                            mm2_ = te.emitm(s_te, fn) if c == NHC - 1 else te.emit(fn)
                        fs["w2_free"] = mm2_
                        fs["b2_free"] = mm2_
                        ve.wait(mm2_)
                        xs = xb[:, dc * TB + boff: dc * TB + boff + W]
                        wbx = ve.emitm(s_ve, lambda e, xs=xs, sl=sl, W=W: e.scalar_tensor_tensor(
                            xs, ps_d[:, sl * 512: sl * 512 + W], 1.0, xs, ALU.mult, ALU.add))
                        fs["d_free"][sl] = wbx
                    xb_state["wr"] = wbx
                    fs["a_free"] = mm2_ if fs["a_free"] is None else fs["a_free"]
                    for c in range(NHC):
                        fs["h4_free"][c] = mm2_
                    fs["h2n_free"] = mm2_
                    fs["stat_free"] = fs["stat_free"]
                    if debug == "xb1blk" and l == 0 and boff == 2304:
                        dstore(s_dbg, [(s_ve.idx, s_ve.count)], dbg[:, :], xb[:])
                        ret_debug[0] = True
                        return
                    if debug == "ffn0blk" and l == 0 and boff == 2304:
                        dstore(s_dbg, [h4_tok, h2n_tok], dbg[:, 0:NDC * 512], h2n[:])
                        dstore(s_dbg, [h4_tok], dbg[:, NDC * 512:], h4[:])
                        ret_debug[0] = True
                        return
            bc = OWN0 * 128 - 1
            kap = bass.AP(tensor=xb.ap().tensor, offset=bc, ap=[[NDC * TB, 128], [TB, NDC], [1, 1]])
            ve.wait(xb_state["wr"], const_tok)
            xb_state["wr"] = ve.emitm(s_ve, lambda e, kap=kap: e.tensor_scalar(
                kap, kap, kill_sb[:, 0:1], None, ALU.mult))
            if debug == "xb1" and l == 0:
                dstore(s_dbg, [xb_state["wr"]], dbg[:, :], xb[:])
                ret_debug[0] = True
            return

        do_layer(0)
        if ret_debug[0]:
            return finish([s_dbg])
        if debug == "xb1":
            pass
        do_layer(1)
        if debug == "xb2":
            dstore(s_dbg, [xb_state["wr"]], dbg[:, :], xb[:])
            return finish([s_dbg])
        if debug == "fullplus":
            dstore(s_dbg, [xb_state["wr"]], dbg[:, 0:NDC * TB], xb[:])

        p.barrier()
        with contextlib.ExitStack() as s4:
            SBo = lambda n, shp: s4.enter_context(nc.sbuf_tensor(n, list(shp), F32))
            xt = SBo("xt", [128, 2 * D])
            a5t = SBo("a5t", [128, 2 * NDC * 128])
            ssum = SBo("ssum", [128, NDC * 128])
            outb = SBo("outb", [128, 2 * D])
            ps_f = s4.enter_context(nc.psum_tensor("ps_f", [128, 1024], F32))
            ot = {"s_free": None, "ps_free": None, "out_free": [None, None], "a5_free": [None, None]}
            sox = [stream_sem(), stream_sem()]
            soa = [stream_sem(), stream_sem()]
            for t in range(TOWN // 128):
                sl = t % 2
                ld.wait(ot["out_free"][sl])
                lx = ld.emitm(sox[sl], lambda e, t=t, sl=sl: e.dma_start(
                    out=xt[:, sl * D:(sl + 1) * D], in_=x_attn[t * 128:(t + 1) * 128, :]))
                ld.wait(a5_store_all, a5_store_all1, ot["a5_free"][sl])
                la = ld.emitm(soa[sl], lambda e, t=t, sl=sl: e.dma_start(
                    out=a5t[:, sl * NDC * 128:(sl + 1) * NDC * 128].rearrange("p (dc c) -> p dc c", c=128),
                    in_=a5_hbm.rearrange("(dc p) q -> p dc q", p=128)[:, :, t * 128:(t + 1) * 128]))
                ve.wait(lx, la, ot["s_free"], xb_state["wr"])
                sv = ve.emitm(s_ve, lambda e, t=t, sl=sl: e.tensor_add(
                    ssum[:].rearrange("p (dc c) -> p dc c", c=128),
                    a5t[:, sl * NDC * 128:(sl + 1) * NDC * 128].rearrange("p (dc c) -> p dc c", c=128),
                    fm_ap(xb, TB, (OWN0 + t) * 128, 128)))
                ot["a5_free"][sl] = sv
                if debug == "o1":
                    dstore(s_dbg, [sv], dbg[:, :], ssum[:])
                    break
                te.wait(sv, i_tok, ot["ps_free"])
                for dc in range(NDC):
                    fn = lambda e, dc=dc: e.transpose(ps_f[:, dc * 128:(dc + 1) * 128], ssum[:, dc * 128:(dc + 1) * 128], ident[:])
                    tf = te.emitm(s_te, fn) if dc == NDC - 1 else te.emit(fn)
                ot["s_free"] = tf
                ve.wait(tf)
                if debug == "o2":
                    ov = ve.emitm(s_ve, lambda e: e.tensor_copy(outb[:, 0:D], ps_f[:]))
                    dstore(s_dbg, [ov], dbg[:, :], outb[:, 0:D])
                    break
                ov = ve.emitm(s_ve, lambda e, sl=sl: e.scalar_tensor_tensor(
                    outb[:, sl * D:(sl + 1) * D], ps_f[:], 1.0, xt[:, sl * D:(sl + 1) * D], ALU.mult, ALU.add))
                ot["ps_free"] = ov
                if debug == "o3":
                    dstore(s_dbg, [ov], dbg[:, :], outb[:, sl * D:(sl + 1) * D])
                    break
                dstore(s_sout[sl], [ov], out_h[t * 128:(t + 1) * 128, :], outb[:, sl * D:(sl + 1) * D])
                ot["out_free"][sl] = (s_sout[sl].idx, s_sout[sl].count)
                if debug == "o4" and t == 1:
                    dstore(s_dbg, [ov], dbg[:, :], outb[:, sl * D:(sl + 1) * D])
                    break
                if debug == "o5" and t == 3:
                    dstore(s_dbg, [ov], dbg[:, :], outb[:, sl * D:(sl + 1) * D])
                    break
                if debug == "o6" and t == 7:
                    dstore(s_dbg, [ov], dbg[:, :], outb[:, sl * D:(sl + 1) * D])
                    break

    if debug in ("o1", "o2", "o3", "o4", "o5", "o6"):
        return finish([s_dbg])
    if debug == "fullplus":
        return finish([s_sout[0], s_sout[1], s_dbg])
    return finish([s_sout[0], s_sout[1]])



_TAB_CACHE = {}


def _rotary_tables_f32():
    if "t" in _TAB_CACHE:
        return _TAB_CACHE["t"]
    import jax
    import jax.numpy as jnp
    cpu = jax.devices("cpu")[0]
    with jax.default_device(cpu):
        half = jnp.arange(0, DH, 2, dtype=jnp.float32)
        inv_freq = 1.0 / 10000.0 ** (half / DH)
        t = jnp.arange(S, dtype=jnp.float32)
        freqs = t[:, None] * inv_freq[None, :]
        freqs = jnp.concatenate([freqs, freqs], axis=-1)
        base = (half + 0.4 * DH) / (1.4 * DH)
        power = (t - S // 2) / float(WIN // 2)
        scale = base[None, :] ** power[:, None]
        scale = jnp.concatenate([scale, scale], axis=-1)
        cos, sin = jnp.cos(freqs), jnp.sin(freqs)
        res = (np.asarray(cos, np.float32), np.asarray(sin, np.float32),
               np.asarray(scale, np.float32))
    _TAB_CACHE["t"] = res
    return res


def _prep_core_inputs(inputs, b, h):
    x = np.asarray(inputs["L0_attn_inputs"][b], dtype=np.float32)
    own = slice(h * TOWN, (h + 1) * TOWN)
    other = slice((1 - h) * TOWN, (2 - h) * TOWN)
    x_attn = np.ascontiguousarray(np.concatenate([x[own], x[other]], axis=0))

    base0 = h * TOWN - OWN0 * 128
    x_bwin = np.zeros((TB, D), dtype=np.float32)
    lo, hi = max(0, base0), min(S, base0 + TB)
    x_bwin[lo - base0: hi - base0] = x[lo:hi]

    cos_f, sin_f, scale_f = _rotary_tables_f32()
    tidx = np.clip(base0 + np.arange(TB), 0, S - 1)
    cos_t = np.ascontiguousarray(cos_f[tidx])
    sin_t = np.ascontiguousarray(sin_f[tidx])
    scale = np.ascontiguousarray(scale_f[tidx])

    def mk_masks(t0):
        n = NT - t0
        m = np.zeros((n, 128, NKB), dtype=np.float32)
        a = np.arange(128)[:, None]
        j = np.arange(NKB)[None, :]
        band = (j < a) | (j > a + WIN)
        for i in range(n):
            t = t0 + i
            abs_start = base0 + (t - 4) * 128
            kill = (j + abs_start) < 0
            m[i][band | kill] = NEG
        return m

    gam = np.asarray(inputs["L0_attn_gamma"], dtype=np.float32)
    bet = np.asarray(inputs["L0_attn_beta"], dtype=np.float32)
    gbat = np.ascontiguousarray(np.stack([gam[0], bet[0], gam[1], bet[1]], axis=1))

    lng = np.zeros((128, 32), dtype=np.float32)
    ln_g = np.asarray(inputs["L0_ffn_ln_g"], dtype=np.float32)
    ln_b = np.asarray(inputs["L0_ffn_ln_b"], dtype=np.float32)
    for l in range(2):
        for dc in range(NDC):
            lng[:, l * 16 + dc * 2 + 0] = ln_g[l, dc * 128:(dc + 1) * 128]
            lng[:, l * 16 + dc * 2 + 1] = ln_b[l, dc * 128:(dc + 1) * 128]

    sigma = np.concatenate([np.arange(32, 64), np.arange(0, 32)])
    eps_pat = np.concatenate([-np.ones(32), np.ones(32)]).astype(np.float32)
    qs = np.asarray(inputs["L0_ffn_q_scale"], dtype=np.float32)
    ks = np.asarray(inputs["L0_ffn_k_scale"], dtype=np.float32)

    return {
        "x_attn": x_attn, "x_bwin": x_bwin,
        "w_uv": np.asarray(inputs["L0_attn_W_uv"], np.float32),
        "b_uv": np.asarray(inputs["L0_attn_b_uv"], np.float32)[None, :],
        "gbat": gbat,
        "w_out": np.asarray(inputs["L0_attn_W_out"], np.float32),
        "b_out": np.asarray(inputs["L0_attn_b_out"], np.float32)[None, :],
        "wqkv": np.asarray(inputs["L0_ffn_wqkv"], np.float32),
        "wo": np.asarray(inputs["L0_ffn_wo"], np.float32),
        "w1": np.asarray(inputs["L0_ffn_w1"], np.float32),
        "b1": np.asarray(inputs["L0_ffn_b1"], np.float32),
        "w2": np.asarray(inputs["L0_ffn_w2"], np.float32),
        "b2": np.asarray(inputs["L0_ffn_b2"], np.float32),
        "lngb": lng,
        "qs_row": qs, "ks_row": ks,
        "qs_sh": np.ascontiguousarray(qs[:, sigma]),
        "ks_sh": np.ascontiguousarray(ks[:, sigma]),
        "eps_row": eps_pat[None, :],
        "cos_t": cos_t, "sin_t": sin_t, "scale_t": scale,
        "masks0": mk_masks(ATTN_T[0]), "masks1": mk_masks(ATTN_T[1]),
        "ident": np.eye(128, dtype=np.float32),
        "kill01": np.full((128, 1), 0.0 if base0 < 0 else 1.0, dtype=np.float32),
    }


def kernel(**inputs):
    nc = build_full()
    in_maps = [_prep_core_inputs(inputs, c // 2, c % 2) for c in range(8)]
    res = run_bass_kernel_spmd(nc, in_maps, core_ids=list(range(8)))
    out = np.zeros((B, S, D), dtype=np.float32)
    for c in range(8):
        b, h = c // 2, c % 2
        out[b, h * TOWN:(h + 1) * TOWN] = res.results[c]["out"]
    return out


# version 06
# speedup vs baseline: 1.0199x; 1.0199x over previous
import math
import contextlib
import numpy as np

import concourse.bass as bass
import concourse.mybir as mybir
from concourse.bass_utils import run_bass_kernel_spmd

F32 = mybir.dt.float32
AF = mybir.ActivationFunctionType
ALU = mybir.AluOpType
AX = mybir.AxisListType

B, S, D = 4, 4096, 1024
NDC = D // 128
E = 2048
SDIM = 128
FUV = 2 * E + SDIM
HEADS, DH = 8, 64
INNER = HEADS * DH
WIN = 512
HID = 2730
HID2 = 2 * HID
NHC = (HID + 127) // 128
EPS_LN = 1e-5
NEG = -3.4028235e38
LN1E4 = math.log(10000.0)
PI = math.pi

TOWN = 2048
NT = 28
TB = NT * 128
OWN0 = 12
BAND = 5
NKB = BAND * 128
QKV_T = {0: 1, 1: 7}
ATTN_T = {0: 5, 1: 11}
FFN_T = {0: 6, 1: 12}

ENGINES = ("sync", "tensor", "vector", "scalar", "gpsimd")


class _Sem:
    def __init__(self, prog, name, dma):
        self.name, self.idx = name, len(prog.sems)
        self.step = 16 if dma else 1
        self.count = 0
        prog.sems.append(self)


class _Queue:
    def __init__(self, prog, engine):
        self.prog, self.engine = prog, engine

    def wait(self, *toks):
        merged = {}
        for tok in toks:
            if tok is None:
                continue
            si, val = tok
            merged[si] = max(merged.get(si, 0), val)
        w = self.prog.engine_waited[self.engine]
        for si, val in merged.items():
            if w.get(si, 0) < val:
                w[si] = val
                self.prog.engine_entries[self.engine].append(("wait", si, val))

    def emit(self, fn):
        self.prog.engine_entries[self.engine].append(("emit", fn))

    def emitm(self, sem, fn):
        sem.count += sem.step
        self.prog.engine_entries[self.engine].append(("emitm", fn, sem.idx, sem.step))
        return (sem.idx, sem.count)


class _Prog:
    def __init__(self, nc):
        self.nc = nc
        self.sems = []
        self.engine_entries = {e: [] for e in ENGINES}
        self.engine_waited = {e: {} for e in ENGINES}
        self._queues = {}

    def sem(self, name, dma=False):
        return _Sem(self, name, dma)

    def q(self, engine):
        if engine not in self._queues:
            self._queues[engine] = _Queue(self, engine)
        return self._queues[engine]

    def barrier(self):
        toks = [(s.idx, s.count) for s in self.sems if s.count]
        for e in ENGINES:
            self.q(e).wait(*toks)

    def build(self):
        nc = self.nc
        with contextlib.ExitStack() as stack:
            handles = [stack.enter_context(nc.semaphore(f"s{i}_{s.name}"))
                       for i, s in enumerate(self.sems)]
            block = stack.enter_context(nc.Block())

            def section(eng_name):
                entries = self.engine_entries[eng_name]
                def body(eng):
                    for e in entries:
                        if e[0] == "wait":
                            eng.wait_ge(handles[e[1]], e[2])
                        elif e[0] == "emit":
                            e[1](eng)
                        else:
                            e[1](eng).then_inc(handles[e[2]], e[3])
                return body

            for e in ENGINES:
                if self.engine_entries[e]:
                    getattr(block, e)(section(e))


def build_full(debug=None):
    nc = bass.Bass("TRN2", target_bir_lowering=False)
    d = lambda n, shp, kind="ExternalInput": nc.dram_tensor(n, list(shp), F32, kind=kind).ap()
    x_attn = d("x_attn", [S, D])
    x_bwin = d("x_bwin", [TB, D])
    w_uv = d("w_uv", [D, FUV])
    b_uv = d("b_uv", [1, FUV])
    gbat = d("gbat", [128, 4])
    w_out = d("w_out", [E, D])
    b_out = d("b_out", [1, D])
    wqkv_h = d("wqkv", [2, D, 3 * INNER])
    wo_h = d("wo", [2, INNER, D])
    w1_h = d("w1", [2, D, HID2])
    b1_h = d("b1", [2, HID2])
    w2_h = d("w2", [2, HID, D])
    b2_h = d("b2", [2, D])
    lngb = d("lngb", [128, 32])
    qs_row = d("qs_row", [2, DH])
    ks_row = d("ks_row", [2, DH])
    qs_sh = d("qs_sh", [2, DH])
    ks_sh = d("ks_sh", [2, DH])
    eps_row = d("eps_row", [1, DH])
    cos_h = d("cos_t", [TB, DH])
    sin_h = d("sin_t", [TB, DH])
    scl_h = d("scale_t", [TB, DH])
    masks_h = {0: d("masks0", [NT - ATTN_T[0], 128, NKB]),
               1: d("masks1", [NT - ATTN_T[1], 128, NKB])}
    ident_h = d("ident", [128, 128])
    kill_h = d("kill01", [128, 1])
    out_h = d("out", [TOWN, D], kind="ExternalOutput")
    xn_hbm = d("xn_scr", [D, S], kind="Internal")
    v_hbm = d("v_scr", [S, E], kind="Internal")
    a5_hbm = d("a5_scr", [D, TOWN], kind="Internal")
    if debug:
        dbg_shp = {"xn": [128, NDC * S], "kq": [128, S + TOWN], "v": [S, E],
                   "a5": [D, TOWN], "xb0": [128, NDC * TB],
                   "xb1": [128, NDC * TB], "xb2": [128, NDC * TB],
                   "xmid0": [128, NDC * TB],
                   "ffn0blk": [128, NDC * 512 + NHC * 512],
                   "xb1blk": [128, NDC * TB],
                   "as0": [128, 256], "as1": [128, 512], "as2": [128, 3 * INNER],
                   "as3": [128, 1024], "as4": [128, NKB], "as5": [128, 512],
                   "as1a": [128, 128], "as1b": [128, 128], "as1c": [128, 128],
                   "as1d": [128, 128], "as1e": [128, 128],
                   "as1x": [128, 128], "as1y": [128, 128], "as1z": [128, 128],
                   "o1": [128, 1024], "o2": [128, 1024], "o3": [128, 1024],
                   "o4": [128, 1024], "o5": [128, 1024], "o6": [128, 1024],
                   "fullplus": [128, NDC * TB + D * 16]}[debug]
        dbg = d("dbg", dbg_shp, kind="ExternalOutput")

    p = _Prog(nc)
    ld, te, ve, sc, gp = (p.q(e) for e in ENGINES)
    s_ld = p.sem("ld", dma=True)
    s_te = p.sem("te")
    s_ve = p.sem("ve")
    s_sc = p.sem("sc")
    s_gp = p.sem("gp")
    s_sxn = p.sem("sxn", dma=True)
    s_sv = [p.sem("sv0", dma=True), p.sem("sv1", dma=True)]
    s_sa5 = [p.sem("sa50", dma=True), p.sem("sa51", dma=True)]
    s_sout = [p.sem("sout0", dma=True), p.sem("sout1", dma=True)]
    s_dbg = p.sem("sdbg", dma=True)
    dms = [p.sem(f"dm{i}", dma=True) for i in range(28)]
    dmi = [0]

    def stream_sem():
        s = dms[dmi[0] % len(dms)]
        dmi[0] += 1
        return s

    ctx = contextlib.ExitStack()
    SB = lambda n, shp: ctx.enter_context(nc.sbuf_tensor(n, list(shp), F32))

    ident = SB("ident_sb", [128, 128])
    ones = SB("ones", [1, 512])
    ones_col = SB("ones_col", [128, 1])
    gb = SB("gb", [128, 4])
    gbq = SB("gbq", [128, 2])
    lng = SB("lng", [128, 32])
    s_const = p.sem("const", dma=True)
    kill_sb = SB("kill_sb", [128, 1])
    ld.emitm(s_const, lambda e: e.dma_start(out=kill_sb[:], in_=kill_h[:, :]))
    ld.emitm(s_const, lambda e: e.dma_start(out=ident[:], in_=ident_h[:, :]))
    ld.emitm(s_const, lambda e: e.dma_start(out=gb[:], in_=gbat[:, :]))
    const_tok = ld.emitm(s_const, lambda e: e.dma_start(out=lng[:], in_=lngb[:, :]))
    i_tok = g_tok = lng_tok = const_tok
    ones_tok = ve.emitm(s_ve, lambda e: e.memset(ones[:], 1.0))
    oc_tok = ve.emitm(s_ve, lambda e: e.memset(ones_col[:], 1.0))
    ve.wait(g_tok)
    gbq_tok = ve.emitm(s_ve, lambda e: e.tensor_scalar_mul(gbq[:], gb[:, 0:2], 1.0 / 4096.0))

    ln_nm = SB("ln_nm", [128, 1])
    ln_ss = SB("ln_ss", [128, 1])
    ln_v1 = SB("ln_v1", [128, 1])
    ln_rs = SB("ln_rs", [128, 1])
    ln_state = {"chain": None, "sq": None}

    def ln_tile(x_ap, dst_ap, deps):
        deps = [t for t in deps if t] + ([ln_state["chain"]] if ln_state["chain"] else [])
        ve.wait(*deps)
        t1 = ve.emitm(s_ve, lambda e: e.reduce_sum(ln_nm[:], x_ap, axis=AX.X, negate=True))
        ve.wait(t1)
        t2 = ve.emitm(s_ve, lambda e: e.tensor_scalar_mul(ln_nm[:], ln_nm[:], 1.0 / D))
        sc.wait(*deps)
        sq_buf = ln_state["sq"]
        t3 = sc.emitm(s_sc, lambda e, sq_buf=sq_buf: e.activation(sq_buf[:], x_ap, AF.Square, accum_out=ln_ss[:]))
        ve.wait(t2, t3)
        t4 = ve.emitm(s_ve, lambda e: e.tensor_scalar_mul(ln_v1[:], ln_ss[:], 1.0 / D))
        t5 = ve.emitm(s_ve, lambda e: e.tensor_mul(ln_rs[:], ln_nm[:], ln_nm[:]))
        ve.wait(t4, t5)
        t6 = ve.emitm(s_ve, lambda e: e.scalar_tensor_tensor(ln_v1[:], ln_rs[:], -1.0, ln_v1[:], ALU.mult, ALU.add))
        ve.wait(t6)
        t7 = ve.emitm(s_ve, lambda e: e.tensor_scalar_add(ln_v1[:], ln_v1[:], EPS_LN))
        sc.wait(t7)
        t8 = sc.emitm(s_sc, lambda e: e.activation(ln_rs[:], ln_v1[:], AF.Sqrt))
        ve.wait(t8)
        t9 = ve.emitm(s_ve, lambda e: e.reciprocal(ln_rs[:], ln_rs[:]))
        ve.wait(t9)
        t10 = ve.emitm(s_ve, lambda e: e.tensor_scalar(dst_ap, x_ap, ln_nm[:, 0:1], ln_rs[:, 0:1], ALU.add, ALU.mult))
        ln_state["chain"] = t10
        return t10

    def mm_one(psum_ap, la, ra, start, stop, gate=(), free=None, mark=True):
        te.wait(*[t for t in gate if t])
        if free is not None:
            te.wait(free)
        if mark:
            return te.emitm(s_te, lambda e: e.matmul(psum_ap, la, ra, start=start, stop=stop))
        te.emit(lambda e: e.matmul(psum_ap, la, ra, start=start, stop=stop))
        return None

    def mm_group(psum_ap, pairs, bias=None, gate=(), free=None):
        te.wait(*[t for t in gate if t])
        if free is not None:
            te.wait(free)
        if bias is not None:
            bl, br = bias
            te.emit(lambda e: e.matmul(psum_ap, bl, br, start=True, stop=False))
        n = len(pairs)
        tok = None
        for i, (la, ra) in enumerate(pairs):
            start = (bias is None) and i == 0
            if i == n - 1:
                tok = te.emitm(s_te, lambda e, la=la, ra=ra, start=start: e.matmul(psum_ap, la, ra, start=start, stop=True))
            else:
                te.emit(lambda e, la=la, ra=ra, start=start: e.matmul(psum_ap, la, ra, start=start, stop=False))
        return tok

    def fm_ap(buf, width, off, n, pn=128):
        F = buf.ap().shape[1]
        return bass.AP(tensor=buf.ap().tensor, offset=off, ap=[[F, pn], [width, NDC], [1, n]])

    def dstore(sem, deps, dst_ap, src_ap):
        gp.wait(*[t for t in deps if t])
        return gp.emitm(sem, lambda e: e.dma_start(out=dst_ap, in_=src_ap))

    def finish(last_dma_sems):
        gp.wait(*[(s.idx, s.count) for s in last_dma_sems if s.count])
        p.build()
        return nc

    kq_tok = [None]
    xn_done = [None]

    phA = contextlib.ExitStack()
    k_fm = phA.enter_context(nc.sbuf_tensor("k_fm", [128, S], F32))
    q_fm = phA.enter_context(nc.sbuf_tensor("q_fm", [128, TOWN], F32))
    with contextlib.ExitStack() as sco:
        SBa = lambda n, shp: sco.enter_context(nc.sbuf_tensor(n, list(shp), F32))
        PSa = lambda n, shp: sco.enter_context(nc.psum_tensor(n, list(shp), F32))
        xn_fm = SBa("xn_fm", [128, NDC * S])
        x_t = SBa("x_t", [128, 2 * D])
        xn_t = SBa("xn_t", [128, D])
        ln_state["sq"] = SBa("ln_sq_a", [128, D])
        bb_base = SBa("bb_base", [1, SDIM])
        bv_eg = SBa("bv_eg", [1, 2 * 512])
        wbase = SBa("wbase", [128, NDC * SDIM])
        base_sb = SBa("base_sb", [128, 2 * 512])
        wv8 = SBa("wv8", [128, NDC * 512])
        v_sb = SBa("v_sb", [128, 2 * 512])
        psT = PSa("psT", [128, 1024])
        psA = PSa("psA", [128, 2 * 512])
        psA2 = PSa("psA2", [128, 2 * 512])

        s_ab1 = stream_sem()
        bv_tok = ld.emitm(s_ab1, lambda e: e.dma_start(out=bb_base[:], in_=b_uv[:, 2 * E:2 * E + SDIM]))

        psT_free = None
        prev_tt = None
        sx = [stream_sem(), stream_sem()]
        x_free = [None, None]
        for t in range(S // 128):
            sl = t % 2
            ld.wait(x_free[sl])
            lt = ld.emitm(sx[sl], lambda e, t=t, sl=sl: e.dma_start(
                out=x_t[:, sl * D:(sl + 1) * D], in_=x_attn[t * 128:(t + 1) * 128, :]))
            lnt = ln_tile(x_t[:, sl * D:(sl + 1) * D], xn_t[:], [lt, prev_tt])
            te.wait(lnt, i_tok)
            if psT_free:
                te.wait(psT_free)
            for dc in range(NDC):
                fn = lambda e, dc=dc: e.transpose(
                    psT[:, dc * 128:(dc + 1) * 128], xn_t[:, dc * 128:(dc + 1) * 128], ident[:])
                tt = te.emitm(s_te, fn) if dc == NDC - 1 else te.emit(fn)
            prev_tt = tt
            ve.wait(tt)
            dst = fm_ap(xn_fm, S, t * 128, 128)
            ev = ve.emitm(s_ve, lambda e, dst=dst: e.tensor_copy(dst, psT[:].rearrange("p (dc c) -> p dc c", c=128)))
            psT_free = ev
            x_free[sl] = lnt
            dstore(s_sxn, [ev], xn_hbm.rearrange("(dc p) t -> p dc t", p=128)[:, :, t * 128:(t + 1) * 128], dst)
        xn_done[0] = ev

        if debug == "xn":
            dstore(s_dbg, [ev], dbg[:, :], xn_fm[:])
            return finish([s_dbg, s_sxn])

        wb_tok = ld.emitm(stream_sem(), lambda e: e.dma_start(
            out=wbase[:], in_=w_uv.rearrange("(dc p) f -> p dc f", p=128)[:, :, 2 * E:2 * E + SDIM]))
        prev_aff = [None, None]
        for tb in range(S // 512):
            sl = tb % 2
            pairs = [(wbase[:, dc * SDIM:(dc + 1) * SDIM],
                      xn_fm[:, dc * S + tb * 512: dc * S + (tb + 1) * 512]) for dc in range(NDC)]
            mm = mm_group(psA[:, sl * 512:(sl + 1) * 512], pairs,
                          bias=(bb_base[:, :], ones[:, :]),
                          gate=[wb_tok, bv_tok, ones_tok, xn_done[0]], free=prev_aff[sl])
            sc.wait(mm)
            if prev_aff[sl]:
                sc.wait(prev_aff[sl])
            sil = sc.emitm(s_sc, lambda e, sl=sl: e.activation(base_sb[:, sl * 512:(sl + 1) * 512], psA[:, sl * 512:(sl + 1) * 512], AF.Silu))
            ve.wait(sil, gbq_tok)
            kev = ve.emitm(s_ve, lambda e, tb=tb, sl=sl: e.tensor_scalar(
                k_fm[:, tb * 512:(tb + 1) * 512], base_sb[:, sl * 512:(sl + 1) * 512], gb[:, 2:3], gb[:, 3:4], ALU.mult, ALU.add))
            last = kev
            if tb < TOWN // 512:
                last = ve.emitm(s_ve, lambda e, tb=tb, sl=sl: e.tensor_scalar(
                    q_fm[:, tb * 512:(tb + 1) * 512], base_sb[:, sl * 512:(sl + 1) * 512], gbq[:, 0:1], gbq[:, 1:2], ALU.mult, ALU.add))
            prev_aff[sl] = last
        kq_tok[0] = (s_ve.idx, s_ve.count)

        if debug == "kq":
            dstore(s_dbg, [kq_tok[0]], dbg[:, 0:S], k_fm[:])
            dstore(s_dbg, [kq_tok[0]], dbg[:, S:S + TOWN], q_fm[:])
            return finish([s_dbg, s_sxn])

        swv = stream_sem()
        sbv = [stream_sem(), stream_sem()]
        bv_free = [None, None]
        wv_free = None
        sil_prev = [None, None]
        st_prev = [None, None]
        for eg in range(E // 512):
            if wv_free:
                ld.wait(wv_free)
            wv_tok = ld.emitm(swv, lambda e, eg=eg: e.dma_start(
                out=wv8[:], in_=w_uv.rearrange("(dc p) f -> p dc f", p=128)[:, :, E + eg * 512: E + (eg + 1) * 512]))
            egsl = eg % 2
            ld.wait(bv_free[egsl])
            bvt = ld.emitm(sbv[egsl], lambda e, eg=eg, egsl=egsl: e.dma_start(
                out=bv_eg[:, egsl * 512:(egsl + 1) * 512], in_=b_uv[:, E + eg * 512:E + (eg + 1) * 512]))
            for t in range(S // 128):
                sl = t % 2
                pairs = [(xn_fm[:, dc * S + t * 128: dc * S + (t + 1) * 128],
                          wv8[:, dc * 512:(dc + 1) * 512]) for dc in range(NDC)]
                mm = mm_group(psA2[:, sl * 512:(sl + 1) * 512], pairs,
                              bias=(ones[:, 0:128], bv_eg[:, egsl * 512:(egsl + 1) * 512]),
                              gate=[wv_tok, bvt, ones_tok, xn_done[0]], free=sil_prev[sl])
                sc.wait(mm)
                if st_prev[sl]:
                    sc.wait(st_prev[sl])
                sil = sc.emitm(s_sc, lambda e, sl=sl: e.activation(v_sb[:, sl * 512:(sl + 1) * 512], psA2[:, sl * 512:(sl + 1) * 512], AF.Silu))
                sil_prev[sl] = sil
                st_prev[sl] = dstore(s_sv[sl], [sil], v_hbm[t * 128:(t + 1) * 128, eg * 512:(eg + 1) * 512], v_sb[:, sl * 512:(sl + 1) * 512])
            wv_free = mm
            bv_free[egsl] = mm

        if debug == "v":
            gp.wait((s_sv[0].idx, s_sv[0].count), (s_sv[1].idx, s_sv[1].count))
            ld.wait((s_sv[0].idx, s_sv[0].count), (s_sv[1].idx, s_sv[1].count))
            bstate = None
            s_bn = stream_sem()
            for t in range(S // 128):
                for eg in range(E // 512):
                    ld.wait(bstate)
                    lt = ld.emitm(s_bn, lambda e, t=t, eg=eg: e.dma_start(
                        out=v_sb[:, 0:512], in_=v_hbm[t * 128:(t + 1) * 128, eg * 512:(eg + 1) * 512]))
                    bstate = dstore(s_dbg, [lt], dbg[t * 128:(t + 1) * 128, eg * 512:(eg + 1) * 512], v_sb[:, 0:512])
            return finish([s_dbg])

    p.barrier()

    xn_store_all = (s_sxn.idx, s_sxn.count)
    v_store_all0 = (s_sv[0].idx, s_sv[0].count)
    v_store_all1 = (s_sv[1].idx, s_sv[1].count)
    with contextlib.ExitStack() as sco:
        SBa = lambda n, shp: sco.enter_context(nc.sbuf_tensor(n, list(shp), F32))
        PSa = lambda n, shp: sco.enter_context(nc.psum_tensor(n, list(shp), F32))
        st_sb = SBa("st_sb", [128, (S // 128) * 512])
        u_sb = SBa("u_sb", [128, (E // 128) * 512])
        xn_sl = SBa("xn_sl", [128, NDC * 512])
        vring = SBa("vring", [128, 2 * S])
        wu = SBa("wu", [128, 2 * NDC * 128])
        wout = SBa("wout", [128, 16 * 128])
        buv_u = SBa("buv_u", [1, E])
        bout_sb = SBa("bout_sb", [1, D])
        a5_sb = SBa("a5_sb", [128, 2 * 512])
        psU = PSa("psU", [128, 2 * 512])
        psS = PSa("psS", [128, 2 * 512])
        psV = PSa("psV", [128, 2 * 512])
        psH = PSa("psH", [128, 2 * 512])

        s_ab2 = stream_sem()
        ld.emitm(s_ab2, lambda e: e.dma_start(out=buv_u[:], in_=b_uv[:, 0:E]))
        bo_tok = ld.emitm(s_ab2, lambda e: e.dma_start(out=bout_sb[:], in_=b_out[:, :]))
        bu_tok = bo_tok

        s_xsl = stream_sem()
        swu = [stream_sem(), stream_sem()]
        swv2 = [stream_sem(), stream_sem()]
        swo = stream_sem()
        wu_free = [None, None]
        v_free = [None, None]
        wo_free = None
        uev_prev = [None, None]
        rl_prev = [None, None]
        uav_prev = [None, None]
        a5_prev = [None, None]
        sq_toks = [None] * (S // 128)

        for qb in range(TOWN // 512):
            ld.wait(xn_store_all)
            ld.wait(wu_free[0], wu_free[1])
            xsl_tok = ld.emitm(s_xsl, lambda e, qb=qb: e.dma_start(
                out=xn_sl[:], in_=xn_hbm.rearrange("(dc p) t -> p dc t", p=128)[:, :, qb * 512:(qb + 1) * 512]))
            for fc in range(E // 128):
                wsl = fc % 2
                if wu_free[wsl]:
                    ld.wait(wu_free[wsl])
                wu_tok = ld.emitm(swu[wsl], lambda e, fc=fc, wsl=wsl: e.dma_start(
                    out=wu[:, wsl * NDC * 128:(wsl + 1) * NDC * 128],
                    in_=w_uv.rearrange("(dc p) f -> p dc f", p=128)[:, :, fc * 128:(fc + 1) * 128]))
                pairs = [(wu[:, wsl * NDC * 128 + dc * 128: wsl * NDC * 128 + (dc + 1) * 128],
                          xn_sl[:, dc * 512:(dc + 1) * 512]) for dc in range(NDC)]
                mm = mm_group(psU[:, wsl * 512:(wsl + 1) * 512], pairs,
                              bias=(buv_u[:, fc * 128:(fc + 1) * 128], ones[:, :]),
                              gate=[wu_tok, xsl_tok, bu_tok, ones_tok], free=uev_prev[wsl])
                wu_free[wsl] = mm
                sc.wait(mm)
                uev = sc.emitm(s_sc, lambda e, fc=fc, wsl=wsl: e.activation(
                    u_sb[:, fc * 512:(fc + 1) * 512], psU[:, wsl * 512:(wsl + 1) * 512], AF.Silu))
                uev_prev[wsl] = uev
            u_tok = (s_sc.idx, s_sc.count)

            for kt in range(S // 128):
                sl = kt % 2
                mm = mm_group(psS[:, sl * 512:(sl + 1) * 512],
                              [(k_fm[:, kt * 128:(kt + 1) * 128], q_fm[:, qb * 512:(qb + 1) * 512])],
                              gate=[kq_tok[0], sq_toks[kt]], free=rl_prev[sl])
                sc.wait(mm)
                rl = sc.emitm(s_sc, lambda e, kt=kt, sl=sl: e.activation(
                    st_sb[:, kt * 512:(kt + 1) * 512], psS[:, sl * 512:(sl + 1) * 512], AF.Relu))
                rl_prev[sl] = rl
                ve.wait(rl)
                sq_toks[kt] = ve.emitm(s_ve, lambda e, kt=kt: e.tensor_mul(
                    st_sb[:, kt * 512:(kt + 1) * 512], st_sb[:, kt * 512:(kt + 1) * 512], st_sb[:, kt * 512:(kt + 1) * 512]))
            st_all = (s_ve.idx, s_ve.count)

            for ec in range(E // 128):
                vsl = ec % 2
                if v_free[vsl]:
                    ld.wait(v_free[vsl])
                ld.wait(v_store_all0, v_store_all1)
                for part in range(4):
                    vt = ld.emitm(swv2[vsl], lambda e, ec=ec, vsl=vsl, part=part: e.dma_start(
                        out=vring[:, vsl * S + part * 1024: vsl * S + (part + 1) * 1024],
                        in_=v_hbm.rearrange("(kt p) e -> p kt e", p=128)[:, part * 8:(part + 1) * 8, ec * 128:(ec + 1) * 128]))
                pairs = [(vring[:, vsl * S + kt * 128: vsl * S + (kt + 1) * 128],
                          st_sb[:, kt * 512:(kt + 1) * 512]) for kt in range(S // 128)]
                mm = mm_group(psV[:, vsl * 512:(vsl + 1) * 512], pairs,
                              gate=[vt, st_all], free=uav_prev[vsl])
                v_free[vsl] = mm
                ve.wait(mm, u_tok)
                uav = ve.emitm(s_ve, lambda e, ec=ec, vsl=vsl: e.tensor_mul(
                    u_sb[:, ec * 512:(ec + 1) * 512], psV[:, vsl * 512:(vsl + 1) * 512], u_sb[:, ec * 512:(ec + 1) * 512]))
                uav_prev[vsl] = uav
            uav_tok = (s_ve.idx, s_ve.count)

            for hb in range(NDC):
                sl = hb % 2
                if wo_free:
                    ld.wait(wo_free)
                wo_tok = ld.emitm(swo, lambda e, hb=hb: e.dma_start(
                    out=wout[:], in_=w_out.rearrange("(ec p) h -> p ec h", p=128)[:, :, hb * 128:(hb + 1) * 128]))
                pairs = [(wout[:, ec * 128:(ec + 1) * 128], u_sb[:, ec * 512:(ec + 1) * 512])
                         for ec in range(E // 128)]
                mm = mm_group(psH[:, sl * 512:(sl + 1) * 512], pairs,
                              bias=(bout_sb[:, hb * 128:(hb + 1) * 128], ones[:, :]),
                              gate=[wo_tok, uav_tok, bo_tok], free=a5_prev[sl])
                wo_free = mm
                ve.wait(mm)
                if a5_prev[sl]:
                    ve.wait(a5_prev[sl])
                ve.wait((s_sa5[sl].idx, s_sa5[sl].count) if s_sa5[sl].count else None)
                a5e = ve.emitm(s_ve, lambda e, sl=sl: e.tensor_copy(
                    a5_sb[:, sl * 512:(sl + 1) * 512], psH[:, sl * 512:(sl + 1) * 512]))
                a5_prev[sl] = a5e
                dstore(s_sa5[sl], [a5e], a5_hbm[hb * 128:(hb + 1) * 128, qb * 512:(qb + 1) * 512],
                       a5_sb[:, sl * 512:(sl + 1) * 512])

        if debug == "a5":
            gp.wait((s_sa5[0].idx, s_sa5[0].count), (s_sa5[1].idx, s_sa5[1].count))
            ld.wait((s_sa5[0].idx, s_sa5[0].count), (s_sa5[1].idx, s_sa5[1].count))
            bstate = None
            s_bn = stream_sem()
            for hb in range(NDC):
                for qb in range(TOWN // 512):
                    ld.wait(bstate)
                    lt = ld.emitm(s_bn, lambda e, hb=hb, qb=qb: e.dma_start(
                        out=a5_sb[:, 0:512], in_=a5_hbm[hb * 128:(hb + 1) * 128, qb * 512:(qb + 1) * 512]))
                    bstate = dstore(s_dbg, [lt], dbg[hb * 128:(hb + 1) * 128, qb * 512:(qb + 1) * 512], a5_sb[:, 0:512])
            return finish([s_dbg])

    phA.close()
    p.barrier()

    a5_store_all = (s_sa5[0].idx, s_sa5[0].count)
    a5_store_all1 = (s_sa5[1].idx, s_sa5[1].count)
    ret_debug = [False]
    with contextlib.ExitStack() as sco:
        SBb = lambda n, shp: sco.enter_context(nc.sbuf_tensor(n, list(shp), F32))
        xb = SBb("xb", [128, NDC * TB])
        xb_state = {"wr": None}

        with contextlib.ExitStack() as s2:
            SBx = lambda n, shp: s2.enter_context(nc.sbuf_tensor(n, list(shp), F32))
            x_t2 = SBx("x_t2", [128, 2 * D])
            xn_t2 = SBx("xn_t2", [128, 2 * D])
            ln_state["sq"] = SBx("ln_sq_b", [128, D])
            psT2 = s2.enter_context(nc.psum_tensor("psT2", [128, 1024], F32))
            psT_free = None
            sx2 = [stream_sem(), stream_sem()]
            x2_free = [None, None]
            for t in range(NT):
                sl = t % 2
                ld.wait(x2_free[sl])
                lt = ld.emitm(sx2[sl], lambda e, t=t, sl=sl: e.dma_start(
                    out=x_t2[:, sl * D:(sl + 1) * D], in_=x_bwin[t * 128:(t + 1) * 128, :]))
                lnt = ln_tile(x_t2[:, sl * D:(sl + 1) * D], xn_t2[:, sl * D:(sl + 1) * D], [lt])
                x2_free[sl] = lnt
                te.wait(lnt, i_tok)
                if psT_free:
                    te.wait(psT_free)
                for dc in range(NDC):
                    fn = lambda e, dc=dc, sl=sl: e.transpose(
                        psT2[:, dc * 128:(dc + 1) * 128], xn_t2[:, sl * D + dc * 128: sl * D + (dc + 1) * 128], ident[:])
                    tt = te.emitm(s_te, fn) if dc == NDC - 1 else te.emit(fn)
                ve.wait(tt)
                ev = ve.emitm(s_ve, lambda e, t=t: e.tensor_copy(
                    fm_ap(xb, TB, t * 128, 128), psT2[:].rearrange("p (dc c) -> p dc c", c=128)))
                psT_free = ev
        xb_tok = ev

        if debug == "xb0":
            dstore(s_dbg, [xb_tok], dbg[:, :], xb[:])
            return finish([s_dbg])

        def do_layer(l):
            QT, AT, FT = QKV_T[l], ATTN_T[l], FFN_T[l]
            p.barrier()
            with contextlib.ExitStack() as s2:
                SBl = lambda n, shp: s2.enter_context(nc.sbuf_tensor(f"{n}_l{l}", list(shp), F32))
                PSl = lambda n, shp: s2.enter_context(nc.psum_tensor(f"{n}_l{l}", list(shp), F32))
                GRP = 2
                NSLOT = 6
                invf = SBl("invf", [1, DH])
                lbase = SBl("lbase", [1, DH])
                coef = SBl("coef", [128, 4 * DH])
                rowbuf = SBl("rowbuf", [1, 2 * DH])
                tcol = SBl("tcol", [128, GRP])
                pwc = SBl("pwc", [128, GRP])
                invf_b = SBl("invf_b", [128, DH])
                lbase_b = SBl("lbase_b", [128, DH])
                frq = SBl("frq", [128, GRP * DH])
                kst_t = SBl("kst_t", [128, GRP * DH])
                kct_t = SBl("kct_t", [128, GRP * DH])
                sarg = SBl("sarg", [128, GRP * DH])
                carg = SBl("carg", [128, GRP * DH])
                sins = SBl("sins", [128, GRP * DH])
                coss = SBl("coss", [128, GRP * DH])
                scl = SBl("scl", [128, GRP * DH])
                scli = SBl("scli", [128, GRP * DH])
                tabs = SBl("tabs", [128, 4 * GRP * DH])
                wq3 = SBl("wq3", [128, NDC * 256])
                qkv_t = SBl("qkv_t", [128, GRP * 3 * INNER])
                sqv = SBl("sqv", [128, 2 * INNER])
                nrm = SBl("nrm", [128, 16])
                qn_sb = SBl("qn_sb", [128, 2 * INNER])
                qhat = SBl("qhat", [128, 2 * INNER])
                qf = SBl("qf", [128, 4 * 128])
                khr = SBl("khr", [128, 4 * NSLOT * 128])
                vr = SBl("vr", [128, NSLOT * INNER])
                msk = SBl("msk", [128, NKB])
                s_sb = SBl("s_sb", [128, NKB])
                p_sb = SBl("p_sb", [128, NKB])
                pt_sb = SBl("pt_sb", [128, NKB])
                sums = SBl("sums", [128, 1])
                nmax = SBl("nmax", [128, 1])
                rcp = SBl("rcp", [128, 1])
                og = SBl("og", [128, GRP * INNER])
                ogf = SBl("ogf", [128, 4 * GRP * 128])
                wo_sl = SBl("wo_sl", [128, 512])
                ps_q = PSl("ps_q", [128, 512])
                ps_s = PSl("ps_s", [128, NKB])
                ps_p = PSl("ps_p", [128, NKB])
                ps_o = PSl("ps_o", [128, 2 * DH])
                ps_w = PSl("ps_w", [128, 512])
                ps_t64 = PSl("ps_t64", [128, 64])

                st = {"tab_tok": None, "ps_q_free": None, "wq_free": None,
                      "sqv_free": None, "qn_free": None, "qhat_free": None,
                      "qf_free": None, "ps_s_free": None, "ps_p_free": None,
                      "ps_o_free": [None, None], "ps_w_free": None,
                      "msk_free": None, "ogf_free": None, "wo_free": [None, None],
                      "s_sb_free": None, "p_sb_free": None, "stat_free": None,
                      "ring_free": {}, "o_written": {}, "ex_prev": None,
                      "rc_prev": None, "sm_prev": None}

                wqs = stream_sem()
                s_ktab, s_mask = stream_sem(), stream_sem()
                wos = [stream_sem(), stream_sem()]
                s_row = stream_sem()

                cts = None
                for j, src in enumerate([qs_row, qs_sh, ks_row, ks_sh]):
                    ld.wait(cts)
                    rt = ld.emitm(s_row, lambda e, src=src: e.dma_start(out=rowbuf[:, DH:2 * DH], in_=src[l:l + 1, :]))
                    mmb = mm_one(ps_t64[:, 0:DH], ones[:, 0:128], rowbuf[:, DH:2 * DH], True, True,
                                 gate=[rt, ones_tok], free=cts)
                    ve.wait(mmb)
                    cts = ve.emitm(s_ve, lambda e, j=j: e.tensor_copy(coef[:, j * DH:(j + 1) * DH], ps_t64[:, 0:DH]))
                ld.wait(cts)
                ep_tok = ld.emitm(s_row, lambda e: e.dma_start(out=rowbuf[:, DH:2 * DH], in_=eps_row[:, :]))
                epb = mm_one(ps_t64[:, 0:DH], ones[:, 0:128], rowbuf[:, DH:2 * DH], True, True,
                             gate=[ep_tok, ones_tok], free=cts)
                ve.wait(epb)
                e1 = ve.emitm(s_ve, lambda e: e.tensor_mul(coef[:, DH:2 * DH], coef[:, DH:2 * DH], ps_t64[:, 0:DH]))
                e2 = ve.emitm(s_ve, lambda e: e.tensor_mul(coef[:, 3 * DH:4 * DH], coef[:, 3 * DH:4 * DH], ps_t64[:, 0:DH]))
                ve.wait(e1, e2)
                e3 = ve.emitm(s_ve, lambda e: e.tensor_scalar_mul(coef[:, 0:DH], coef[:, 0:DH], 8.0))
                e4 = ve.emitm(s_ve, lambda e: e.tensor_scalar_mul(coef[:, DH:2 * DH], coef[:, DH:2 * DH], 8.0))
                coef_tok = e4

                def build_tables(g0, ntile):
                    ld.wait(st.get("ktab_free"))
                    lt1 = ld.emitm(s_ktab, lambda e, g0=g0, ntile=ntile: e.dma_start(
                        out=coss[:, 0:ntile * DH],
                        in_=cos_h.rearrange("(t p) c -> p t c", p=128)[:, g0:g0 + ntile, :]))
                    lt2 = ld.emitm(s_ktab, lambda e, g0=g0, ntile=ntile: e.dma_start(
                        out=sins[:, 0:ntile * DH],
                        in_=sin_h.rearrange("(t p) c -> p t c", p=128)[:, g0:g0 + ntile, :]))
                    lt3 = ld.emitm(s_ktab, lambda e, g0=g0, ntile=ntile: e.dma_start(
                        out=scl[:, 0:ntile * DH],
                        in_=scl_h.rearrange("(t p) c -> p t c", p=128)[:, g0:g0 + ntile, :]))
                    ve.wait(lt3, st["tab_tok"])
                    rc1 = ve.emitm(s_ve, lambda e, ntile=ntile: e.reciprocal(scli[:, 0:ntile * DH], scl[:, 0:ntile * DH]))
                    for i in range(ntile):
                        ve.wait(rc1, coef_tok)
                        tA = []
                        for dst, raw, sclx in ((0, coss, scl), (1, sins, scl), (2, coss, scli), (3, sins, scli)):
                            tA.append(ve.emitm(s_ve, lambda e, dst=dst, raw=raw, sclx=sclx, i=i: e.tensor_mul(
                                tabs[:, (dst * GRP + i) * DH:(dst * GRP + i + 1) * DH], raw[:, i * DH:(i + 1) * DH], sclx[:, i * DH:(i + 1) * DH])))
                        ve.wait(*tA)
                        tB = []
                        for dst in range(4):
                            tB.append(ve.emitm(s_ve, lambda e, dst=dst, i=i: e.tensor_mul(
                                tabs[:, (dst * GRP + i) * DH:(dst * GRP + i + 1) * DH],
                                tabs[:, (dst * GRP + i) * DH:(dst * GRP + i + 1) * DH], coef[:, dst * DH:(dst + 1) * DH])))
                        st["tab_tok"] = tB[-1]
                    st["ktab_free"] = (s_ve.idx, s_ve.count)
                    return (s_ve.idx, s_ve.count)

                def qkv_group(g0, ntile):
                    for ch in range(6):
                        ld.wait(st["wq_free"])
                        wt = ld.emitm(wqs, lambda e, ch=ch: e.dma_start(
                            out=wq3[:], in_=wqkv_h.rearrange("l (dc p) f -> l p dc f", p=128)[l][:, :, ch * 256:(ch + 1) * 256]))
                        for i in range(ntile):
                            t = g0 + i
                            pairs = []
                            for dc in range(NDC):
                                off = dc * TB + t * 128
                                lhs = bass.AP(tensor=xb.ap().tensor, offset=off if dc < 4 else off - 1,
                                              ap=[[NDC * TB, 128], [1, 128]])
                                pairs.append((lhs, wq3[:, dc * 256:(dc + 1) * 256]))
                            mm = mm_group(ps_q[:, 0:256], pairs,
                                          gate=[wt, xb_tok, xb_state["wr"]], free=st["ps_q_free"])
                            ve.wait(mm)
                            cp = ve.emitm(s_ve, lambda e, i=i, ch=ch: e.tensor_copy(
                                qkv_t[:, i * 3 * INNER + ch * 256: i * 3 * INNER + (ch + 1) * 256],
                                ps_q[:, 0:256]))
                            st["ps_q_free"] = cp
                        st["wq_free"] = mm
                    return (s_ve.idx, s_ve.count)

                def prep_tile(i, t, qkv_cp):
                    base = i * 3 * INNER
                    qk = qkv_t[:, base: base + 2 * INNER]
                    sc.wait(qkv_cp, st["sqv_free"])
                    sq1 = sc.emitm(s_sc, lambda e, qk=qk: e.activation(sqv[:], qk, AF.Square))
                    ve.wait(sq1)
                    n1 = ve.emitm(s_ve, lambda e: e.reduce_sum(nrm[:, :].unsqueeze(2), sqv[:].rearrange("p (s c) -> p s c", c=DH), axis=AX.X))
                    sc.wait(n1)
                    n2 = sc.emitm(s_sc, lambda e: e.activation(nrm[:], nrm[:], AF.Sqrt))
                    ve.wait(n2)
                    n3 = ve.emitm(s_ve, lambda e: e.tensor_scalar_max(nrm[:], nrm[:], 1e-12))
                    ve.wait(n3)
                    n4 = ve.emitm(s_ve, lambda e: e.reciprocal(nrm[:], nrm[:]))
                    ve.wait(n4, st["qn_free"])
                    rb = nrm[:, :].unsqueeze(2).broadcast_to((128, 16, DH))
                    qn = ve.emitm(s_ve, lambda e, qk=qk, rb=rb: e.tensor_tensor(
                        qn_sb[:].rearrange("p (s c) -> p s c", c=DH), qk.rearrange("p (s c) -> p s c", c=DH), rb, ALU.mult))
                    ve.wait(qn, st["tab_tok"], st["qhat_free"])
                    outs = []
                    for half, (Aj, Bj) in ((0, (0, 1)), (1, (2, 3))):
                        A_b = bass.AP(tensor=tabs.ap().tensor, offset=(Aj * GRP + i) * DH,
                                      ap=[[4 * GRP * DH, 128], [0, HEADS], [1, DH]])
                        B_b = bass.AP(tensor=tabs.ap().tensor, offset=(Bj * GRP + i) * DH,
                                      ap=[[4 * GRP * DH, 128], [0, HEADS], [32, 2], [1, 32]])
                        seg = qn_sb[:, half * INNER:(half + 1) * INNER]
                        h1 = ve.emitm(s_ve, lambda e, seg=seg, A_b=A_b, half=half: e.tensor_tensor(
                            qhat[:, half * INNER:(half + 1) * INNER].rearrange("p (s c) -> p s c", c=DH),
                            seg.rearrange("p (s c) -> p s c", c=DH), A_b, ALU.mult))
                        swap = bass.AP(tensor=qn_sb.ap().tensor, offset=half * INNER + 32,
                                       ap=[[2 * INNER, 128], [DH, HEADS], [-32, 2], [1, 32]])
                        h2 = ve.emitm(s_ve, lambda e, swap=swap, B_b=B_b, half=half: e.tensor_tensor(
                            sqv[:, half * INNER:(half + 1) * INNER].rearrange("p (s h c) -> p s h c", h=2, c=32),
                            swap, B_b, ALU.mult))
                        ve.wait(h1, h2)
                        outs.append(ve.emitm(s_ve, lambda e, half=half: e.tensor_add(
                            qhat[:, half * INNER:(half + 1) * INNER], qhat[:, half * INNER:(half + 1) * INNER], sqv[:, half * INNER:(half + 1) * INNER])))
                    st["qn_free"] = outs[-1]
                    st["sqv_free"] = outs[-1]
                    st["qhat_free"] = None
                    slot = t % NSLOT
                    te.wait(outs[0], outs[1], i_tok, st["qf_free"], st["ring_free"].get(slot), st["ps_p_free"])
                    for fc in range(4):
                        fn = lambda e, fc=fc: e.transpose(ps_p[:, fc * 128:(fc + 1) * 128], qhat[:, fc * 128:(fc + 1) * 128], ident[:])
                        tq = te.emitm(s_te, fn) if fc == 3 else te.emit(fn)
                    ve.wait(tq)
                    c1 = ve.emitm(s_ve, lambda e: e.tensor_copy(qf[:], ps_p[:, 0:512]))
                    te.wait(c1)
                    for fc in range(4):
                        fn = lambda e, fc=fc: e.transpose(ps_p[:, fc * 128:(fc + 1) * 128], qhat[:, INNER + fc * 128: INNER + (fc + 1) * 128], ident[:])
                        tk = te.emitm(s_te, fn) if fc == 3 else te.emit(fn)
                    st["qhat_free"] = tk
                    ve.wait(tk)
                    c2 = ve.emitm(s_ve, lambda e, slot=slot: e.tensor_copy(
                        bass.AP(tensor=khr.ap().tensor, offset=slot * 128,
                                ap=[[4 * NSLOT * 128, 128], [NSLOT * 128, 4], [1, 128]]),
                        ps_p[:, 0:512].rearrange("p (fc c) -> p fc c", c=128)))
                    st["ps_p_free"] = c2
                    v_cp = ve.emitm(s_ve, lambda e, i=i, slot=slot: e.tensor_copy(
                        vr[:, slot * INNER:(slot + 1) * INNER], qkv_t[:, i * 3 * INNER + 2 * INNER: (i + 1) * 3 * INNER]))
                    return c1

                def attn_tile(t, c1):
                    mi = t - AT
                    ld.wait(st["msk_free"])
                    mt = ld.emitm(s_mask, lambda e, mi=mi: e.dma_start(out=msk[:], in_=masks_h[l][mi]))
                    ring_all = (s_ve.idx, s_ve.count)
                    slots = [(t - 4 + j) % NSLOT for j in range(BAND)]
                    sm = None
                    for h in range(HEADS):
                        fc, plo = h // 2, (h % 2) * DH
                        te.wait(c1, ring_all, st["ps_s_free"])
                        for j, sl_ in enumerate(slots):
                            la = qf[plo:plo + DH, fc * 128:(fc + 1) * 128]
                            ra = khr[plo:plo + DH, fc * NSLOT * 128 + sl_ * 128: fc * NSLOT * 128 + (sl_ + 1) * 128]
                            fn = lambda e, la=la, ra=ra, j=j: e.matmul(
                                ps_s[:, j * 128:(j + 1) * 128], la, ra, start=True, stop=True)
                            mmS = te.emitm(s_te, fn) if j == BAND - 1 else te.emit(fn)
                        ve.wait(mmS, mt, st["ex_prev"])
                        sm = ve.emitm(s_ve, lambda e: e.tensor_add(s_sb[:], ps_s[:], msk[:]))
                        st["ps_s_free"] = sm
                        ve.wait(sm, st["ex_prev"])
                        rm = ve.emitm(s_ve, lambda e: e.reduce_max(nmax[:], s_sb[:], axis=AX.X, negate=True))
                        sc.wait(rm, st["rc_prev"], st["p_sb_free"])
                        ex = sc.emitm(s_sc, lambda e: e.activation(p_sb[:], s_sb[:], AF.Exp, bias=nmax[:, 0:1], accum_out=sums[:]))
                        st["ex_prev"] = ex
                        ve.wait(ex)
                        rc = ve.emitm(s_ve, lambda e: e.reciprocal(rcp[:], sums[:]))
                        st["rc_prev"] = rc
                        te.wait(ex, st["ps_p_free"])
                        for j in range(BAND):
                            fn = lambda e, j=j: e.transpose(ps_p[:, j * 128:(j + 1) * 128], p_sb[:, j * 128:(j + 1) * 128], ident[:])
                            tp = te.emitm(s_te, fn) if j == BAND - 1 else te.emit(fn)
                        st["p_sb_free"] = tp
                        ve.wait(tp)
                        ptc = ve.emitm(s_ve, lambda e: e.tensor_copy(pt_sb[:], ps_p[:]))
                        st["ps_p_free"] = ptc
                        psl = h % 2
                        pairs = [(pt_sb[:, j * 128:(j + 1) * 128],
                                  vr[:, slots[j] * INNER + h * DH: slots[j] * INNER + (h + 1) * DH])
                                 for j in range(BAND)]
                        mmO = mm_group(ps_o[:, psl * DH:(psl + 1) * DH], pairs, gate=[ptc, ring_all],
                                       free=st["ps_o_free"][psl])
                        ve.wait(mmO, rc, st.get("og_free"))
                        oev = ve.emitm(s_ve, lambda e, t=t, h=h, psl=psl: e.tensor_scalar(
                            og[:, (t % GRP) * INNER + h * DH: (t % GRP) * INNER + (h + 1) * DH],
                            ps_o[:, psl * DH:(psl + 1) * DH], rcp[:, 0:1], None, ALU.mult))
                        st["ps_o_free"][psl] = oev
                    st["msk_free"] = sm
                    st["qf_free"] = (s_te.idx, s_te.count)
                    st["o_written"][t % GRP] = oev
                    st["ring_free"][(t + 2) % NSLOT] = (s_te.idx, s_te.count)
                    return oev

                def wo_group(g0, ntile, qkv_guard):
                    for i in range(ntile):
                        slot = (g0 + i) % GRP
                        te.wait(i_tok, st["ogf_free"], st["o_written"].get(slot), st["ps_p_free"])
                        for fc in range(4):
                            fn = lambda e, slot=slot, fc=fc: e.transpose(
                                ps_p[:, fc * 128:(fc + 1) * 128], og[:, slot * INNER + fc * 128: slot * INNER + (fc + 1) * 128], ident[:])
                            to_ = te.emitm(s_te, fn) if fc == 3 else te.emit(fn)
                        ve.wait(to_)
                        cp = ve.emitm(s_ve, lambda e, i=i: e.tensor_copy(
                            bass.AP(tensor=ogf.ap().tensor, offset=i * 128,
                                    ap=[[4 * GRP * 128, 128], [GRP * 128, 4], [1, 128]]),
                            ps_p[:, 0:512].rearrange("p (fc c) -> p fc c", c=128)))
                        st["ps_p_free"] = cp
                    st["og_free"] = (s_te.idx, s_te.count)
                    ogf_tok = (s_ve.idx, s_ve.count)
                    W = ntile * 128
                    wb = None
                    for dc in range(NDC):
                        ld.wait(st["wo_free"][0])
                        wt = ld.emitm(wos[0], lambda e, dc=dc: e.dma_start(
                            out=wo_sl[:],
                            in_=wo_h.rearrange("l (ic p) d -> l p ic d", p=128)[l][:, :, dc * 128:(dc + 1) * 128]))
                        pairs = [(wo_sl[:, ic * 128:(ic + 1) * 128],
                                  ogf[:, ic * GRP * 128: ic * GRP * 128 + W]) for ic in range(4)]
                        mm = mm_group(ps_w[:, 0:W], pairs, gate=[wt, ogf_tok, qkv_guard], free=st["ps_w_free"])
                        st["wo_free"][0] = mm
                        ve.wait(mm)
                        xs = xb[:, dc * TB + g0 * 128: dc * TB + g0 * 128 + W]
                        wb = ve.emitm(s_ve, lambda e, xs=xs, W=W: e.scalar_tensor_tensor(
                            xs, ps_w[:, 0:W], 1.0, xs, ALU.mult, ALU.add))
                        st["ps_w_free"] = wb
                    st["ogf_free"] = mm
                    xb_state["wr"] = wb
                    return wb

                if debug == "as0" and l == 0:
                    dstore(s_dbg, [coef_tok], dbg[:, :], coef[:])
                    ret_debug[0] = True
                    return
                ntile_all = NT - QT
                groups = [(QT + GRP * g, min(GRP, NT - (QT + GRP * g))) for g in range((ntile_all + GRP - 1) // GRP)]
                pend = None
                for (g0, ntile) in groups:
                    r_ = build_tables(g0, ntile)
                    if r_ == "STOP":
                        dstore(s_dbg, st["_stop"], dbg[:, :], sarg[:, 0:128])
                        ret_debug[0] = True
                        return
                    if debug == "as1" and l == 0:
                        dstore(s_dbg, [st["tab_tok"]], dbg[:, :], tabs[:])
                        ret_debug[0] = True
                        return
                    qkv_cp = qkv_group(g0, ntile)
                    if debug == "as2" and l == 0:
                        dstore(s_dbg, [qkv_cp], dbg[:, :], qkv_t[:, 0:3 * INNER])
                        ret_debug[0] = True
                        return
                    if pend:
                        wo_group(pend[0], pend[1], qkv_cp)
                        pend = None
                    a_start = None
                    for i in range(ntile):
                        t = g0 + i
                        c1 = prep_tile(i, t, qkv_cp)
                        if debug == "as3" and l == 0:
                            dstore(s_dbg, [c1, (kernel_sve := None) or (s_ve.idx, s_ve.count)], dbg[:, :], qhat[:])
                            ret_debug[0] = True
                            return
                        if t >= AT:
                            if a_start is None:
                                a_start = t
                            attn_tile(t, c1)
                            if debug == "as4" and l == 0:
                                dstore(s_dbg, [(s_ve.idx, s_ve.count)], dbg[:, :], p_sb[:])
                                ret_debug[0] = True
                                return
                    if a_start is not None:
                        pend = (a_start, g0 + ntile - a_start)
                    if debug == "as5" and l == 0 and pend and pend[0] is not None and g0 >= AT:
                        wo_group(pend[0], pend[1], None)
                        dstore(s_dbg, [xb_state["wr"]], dbg[:, 0:512], xb[:, 0:512])
                        ret_debug[0] = True
                        return
                if pend:
                    wo_group(pend[0], pend[1], None)
                    pend = None
                if l == 1:
                    bc = OWN0 * 128 - 1
                    kap = bass.AP(tensor=xb.ap().tensor, offset=bc, ap=[[NDC * TB, 128], [TB, NDC], [1, 1]])
                    ve.wait(xb_state["wr"], const_tok)
                    xb_state["wr"] = ve.emitm(s_ve, lambda e, kap=kap: e.tensor_scalar(
                        kap, kap, kill_sb[:, 0:1], None, ALU.mult))

            if debug == "xmid0" and l == 0:
                dstore(s_dbg, [xb_state["wr"]], dbg[:, :], xb[:])
                ret_debug[0] = True
                return

            p.barrier()
            with contextlib.ExitStack() as s3:
                SBf = lambda n, shp: s3.enter_context(nc.sbuf_tensor(f"{n}_f{l}", list(shp), F32))
                PSf = lambda n, shp: s3.enter_context(nc.psum_tensor(f"{n}_f{l}", list(shp), F32))
                h2n = SBf("h2n", [128, NDC * 512])
                h4 = SBf("h4", [128, NHC * 512])
                sqf = SBf("sqf", [128, 512])
                gel = sqf
                stat = SBf("stat", [1, 2 * 512])
                nmb = SBf("nmb", [128, 512])
                rsb = SBf("rsb", [128, 512])
                w1a = SBf("w1a", [128, NDC * 128])
                w1g = SBf("w1g", [128, NDC * 128])
                w2t = SBf("w2t", [128, NHC * 128])
                b1s = SBf("b1s", [1, 2 * 128])
                b2s = SBf("b2s", [1, 2 * 128])
                ps_st = PSf("ps_st", [1, 2 * 512])
                ps_bc = PSf("ps_bc", [128, 2 * 512])
                ps_a = PSf("ps_a", [128, 512])
                ps_g = PSf("ps_g", [128, 512])
                ps_d = PSf("ps_d", [128, 2 * 512])

                w1s, w2s, b1st = stream_sem(), stream_sem(), stream_sem()
                b2st = stream_sem()
                if debug == "ffn0blk":
                    ve.emitm(s_ve, lambda e: e.memset(h4[:], 0.0))
                fs = {"stat_free": None, "sq_free": None, "bc_free": None,
                      "nmb_free": None, "h2n_free": None, "a_free": None,
                      "g_free": None, "gel_free": None, "h4_free": {},
                      "d_free": [None, None], "w1_free": None, "w2_free": None,
                      "b1_free": None, "w2_mm": None}

                blocks = []
                off = FT * 128
                while off < NT * 128:
                    w = min(512, NT * 128 - off)
                    blocks.append((off, w))
                    off += w

                for (boff, W) in reversed(blocks):
                    def h2ap(dc, boff=boff, W=W):
                        off2 = dc * TB + boff
                        return bass.AP(tensor=xb.ap().tensor, offset=off2 if dc < 4 else off2 - 1,
                                       ap=[[NDC * TB, 128], [1, W]])
                    for dc in range(NDC):
                        mms = mm_one(ps_st[0:1, 0:W], ones_col[:, 0:1], h2ap(dc), dc == 0, dc == NDC - 1,
                                     gate=[oc_tok, xb_tok, xb_state["wr"]],
                                     free=fs["stat_free"] if dc == 0 else None, mark=dc == NDC - 1)
                    for dc in range(NDC):
                        sc.wait(fs["sq_free"], fs["gel_free"], xb_state["wr"])
                        sq = sc.emitm(s_sc, lambda e, dc=dc, W=W, h2ap=h2ap: e.activation(sqf[:, 0:W], h2ap(dc), AF.Square))
                        mmq = mm_one(ps_st[0:1, 512:512 + W], ones_col[:, 0:1], sqf[:, 0:W], dc == 0, dc == NDC - 1,
                                     gate=[sq, oc_tok], free=fs["stat_free"] if dc == 0 else None,
                                     mark=True)
                        fs["sq_free"] = mmq
                    ve.wait(mms, mmq, fs["bc_free"])
                    f1 = ve.emitm(s_ve, lambda e, W=W: e.tensor_scalar_mul(stat[0:1, 0:W], ps_st[0:1, 0:W], -1.0 / D))
                    f2 = ve.emitm(s_ve, lambda e, W=W: e.tensor_scalar_mul(stat[0:1, 512:512 + W], ps_st[0:1, 512:512 + W], 1.0 / D))
                    fs["stat_free"] = f2
                    ve.wait(f1, f2)
                    f3 = ve.emitm(s_ve, lambda e, W=W: e.scalar_tensor_tensor(
                        stat[0:1, 512:512 + W], stat[0:1, 0:W], -1.0, stat[0:1, 512:512 + W], ALU.mult, ALU.add))
                    ve.wait(f3)
                    f2b = ve.emitm(s_ve, lambda e, W=W: e.tensor_scalar_mul(stat[0:1, 512:512 + W], ps_st[0:1, 512:512 + W], 1.0 / D))
                    ve.wait(f2b)
                    f4 = ve.emitm(s_ve, lambda e, W=W: e.tensor_mul(sqf[0:1, 0:W], stat[0:1, 0:W], stat[0:1, 0:W]))
                    ve.wait(f4)
                    f5 = ve.emitm(s_ve, lambda e, W=W: e.tensor_sub(stat[0:1, 512:512 + W], stat[0:1, 512:512 + W], sqf[0:1, 0:W]))
                    ve.wait(f5)
                    f6 = ve.emitm(s_ve, lambda e, W=W: e.tensor_scalar_add(stat[0:1, 512:512 + W], stat[0:1, 512:512 + W], EPS_LN))
                    sc.wait(f6)
                    f7 = sc.emitm(s_sc, lambda e, W=W: e.activation(stat[0:1, 512:512 + W], stat[0:1, 512:512 + W], AF.Sqrt))
                    ve.wait(f7)
                    f8 = ve.emitm(s_ve, lambda e, W=W: e.reciprocal(stat[0:1, 512:512 + W], stat[0:1, 512:512 + W]))
                    m1 = mm_one(ps_bc[:, 0:W], ones[:, 0:128], stat[0:1, 0:W], True, True,
                                gate=[f8, ones_tok], free=fs["nmb_free"])
                    m2 = mm_one(ps_bc[:, 512:512 + W], ones[:, 0:128], stat[0:1, 512:512 + W], True, True,
                                gate=[f8, ones_tok], free=fs["nmb_free"])
                    ve.wait(m1, m2)
                    g1 = ve.emitm(s_ve, lambda e, W=W: e.tensor_copy(nmb[:, 0:W], ps_bc[:, 0:W]))
                    g2 = ve.emitm(s_ve, lambda e, W=W: e.tensor_copy(rsb[:, 0:W], ps_bc[:, 512:512 + W]))
                    fs["bc_free"] = g2
                    h2n_toks = []
                    for dc in range(NDC):
                        ve.wait(g1, g2, fs["h2n_free"])
                        q1 = ve.emitm(s_ve, lambda e, dc=dc, W=W, h2ap=h2ap: e.tensor_add(h2n[:, dc * 512: dc * 512 + W], h2ap(dc), nmb[:, 0:W]))
                        ve.wait(q1)
                        q2 = ve.emitm(s_ve, lambda e, dc=dc, W=W: e.tensor_mul(h2n[:, dc * 512: dc * 512 + W], h2n[:, dc * 512: dc * 512 + W], rsb[:, 0:W]))
                        ve.wait(q2, lng_tok)
                        q3 = ve.emitm(s_ve, lambda e, dc=dc, W=W: e.tensor_scalar(
                            h2n[:, dc * 512: dc * 512 + W], h2n[:, dc * 512: dc * 512 + W],
                            lng[:, l * 16 + dc * 2: l * 16 + dc * 2 + 1], lng[:, l * 16 + dc * 2 + 1: l * 16 + dc * 2 + 2],
                            ALU.mult, ALU.add))
                        h2n_toks.append(q3)
                    h2n_tok = (s_ve.idx, s_ve.count)
                    fs["nmb_free"] = h2n_tok

                    for c in range(NHC):
                        cw = min(128, HID - c * 128)
                        ld.wait(fs["w1_free"])
                        wt1 = ld.emitm(w1s, lambda e, c=c, cw=cw: e.dma_start(
                            out=w1a[:, 0:NDC * cw],
                            in_=w1_h.rearrange("l (dc p) f -> l p dc f", p=128)[l][:, :, c * 128: c * 128 + cw]))
                        wt2 = ld.emitm(w1s, lambda e, c=c, cw=cw: e.dma_start(
                            out=w1g[:, 0:NDC * cw],
                            in_=w1_h.rearrange("l (dc p) f -> l p dc f", p=128)[l][:, :, HID + c * 128: HID + c * 128 + cw]))
                        ld.wait(fs["b1_free"])
                        bt1 = ld.emitm(b1st, lambda e, c=c, cw=cw: e.dma_start(
                            out=b1s[:, 0:cw], in_=b1_h[l:l + 1, c * 128: c * 128 + cw]))
                        bt2 = ld.emitm(b1st, lambda e, c=c, cw=cw: e.dma_start(
                            out=b1s[:, 128:128 + cw], in_=b1_h[l:l + 1, HID + c * 128: HID + c * 128 + cw]))
                        pa = [(w1a[:, dc * cw:(dc + 1) * cw], h2n[:, dc * 512: dc * 512 + W]) for dc in range(NDC)]
                        mma = mm_group(ps_a[0:cw, 0:W], pa, bias=(b1s[:, 0:cw], ones[:, 0:W]),
                                       gate=[wt2, bt2, h2n_tok, ones_tok], free=fs["a_free"])
                        pg = [(w1g[:, dc * cw:(dc + 1) * cw], h2n[:, dc * 512: dc * 512 + W]) for dc in range(NDC)]
                        mmg = mm_group(ps_g[0:cw, 0:W], pg, bias=(b1s[:, 128:128 + cw], ones[:, 0:W]),
                                       gate=[wt2, bt2, h2n_tok, ones_tok], free=fs["g_free"])
                        fs["w1_free"] = mmg
                        fs["b1_free"] = mmg
                        sc.wait(mmg, fs["gel_free"], fs["sq_free"])
                        ge = sc.emitm(s_sc, lambda e, cw=cw, W=W: e.activation(gel[0:cw, 0:W], ps_g[0:cw, 0:W], AF.Gelu))
                        fs["g_free"] = ge
                        fs["sq_free"] = ge
                        ve.wait(ge, mma, fs["h4_free"].get(c))
                        hv = ve.emitm(s_ve, lambda e, c=c, cw=cw, W=W: e.tensor_tensor(
                            h4[0:cw, c * 512: c * 512 + W], ps_a[0:cw, 0:W], gel[0:cw, 0:W], ALU.mult))
                        fs["a_free"] = hv
                        fs["gel_free"] = hv
                        fs["h4_free"][c] = None
                    h4_tok = (s_ve.idx, s_ve.count)

                    for dc in range(NDC):
                        sl = dc % 2
                        ld.wait(fs["w2_free"])
                        wt3 = ld.emitm(w2s, lambda e, dc=dc: e.dma_start(
                            out=w2t[:, 0:21 * 128].rearrange("p (c d) -> p c d", d=128),
                            in_=w2_h[l, 0:21 * 128, dc * 128:(dc + 1) * 128].rearrange("(c p) d -> p c d", p=128)))
                        wt4 = ld.emitm(w2s, lambda e, dc=dc: e.dma_start(
                            out=w2t[0:42, 21 * 128:22 * 128],
                            in_=w2_h[l, 21 * 128:HID, dc * 128:(dc + 1) * 128]))
                        ld.wait(fs.get("b2_free"))
                        b2t = ld.emitm(b2st, lambda e, dc=dc, sl=sl: e.dma_start(
                            out=b2s[:, sl * 128:(sl + 1) * 128], in_=b2_h[l:l + 1, dc * 128:(dc + 1) * 128]))
                        te.wait(wt4, h4_tok, b2t, fs["d_free"][sl])
                        te.emit(lambda e, dc=dc, sl=sl, W=W: e.matmul(
                            ps_d[:, sl * 512: sl * 512 + W], b2s[:, sl * 128:(sl + 1) * 128], ones[:, 0:W], start=True, stop=False))
                        for c in range(NHC):
                            cw = min(128, HID - c * 128)
                            fn = lambda e, c=c, cw=cw, dc=dc, sl=sl, W=W: e.matmul(
                                ps_d[:, sl * 512: sl * 512 + W], w2t[0:cw, c * 128:(c + 1) * 128],
                                h4[0:cw, c * 512: c * 512 + W], start=False, stop=c == NHC - 1)
                            mm2_ = te.emitm(s_te, fn) if c == NHC - 1 else te.emit(fn)
                        fs["w2_free"] = mm2_
                        fs["b2_free"] = mm2_
                        ve.wait(mm2_)
                        xs = xb[:, dc * TB + boff: dc * TB + boff + W]
                        wbx = ve.emitm(s_ve, lambda e, xs=xs, sl=sl, W=W: e.scalar_tensor_tensor(
                            xs, ps_d[:, sl * 512: sl * 512 + W], 1.0, xs, ALU.mult, ALU.add))
                        fs["d_free"][sl] = wbx
                    xb_state["wr"] = wbx
                    fs["a_free"] = mm2_ if fs["a_free"] is None else fs["a_free"]
                    for c in range(NHC):
                        fs["h4_free"][c] = mm2_
                    fs["h2n_free"] = mm2_
                    fs["stat_free"] = fs["stat_free"]
                    if debug == "xb1blk" and l == 0 and boff == 2304:
                        dstore(s_dbg, [(s_ve.idx, s_ve.count)], dbg[:, :], xb[:])
                        ret_debug[0] = True
                        return
                    if debug == "ffn0blk" and l == 0 and boff == 2304:
                        dstore(s_dbg, [h4_tok, h2n_tok], dbg[:, 0:NDC * 512], h2n[:])
                        dstore(s_dbg, [h4_tok], dbg[:, NDC * 512:], h4[:])
                        ret_debug[0] = True
                        return
            bc = OWN0 * 128 - 1
            kap = bass.AP(tensor=xb.ap().tensor, offset=bc, ap=[[NDC * TB, 128], [TB, NDC], [1, 1]])
            ve.wait(xb_state["wr"], const_tok)
            xb_state["wr"] = ve.emitm(s_ve, lambda e, kap=kap: e.tensor_scalar(
                kap, kap, kill_sb[:, 0:1], None, ALU.mult))
            if debug == "xb1" and l == 0:
                dstore(s_dbg, [xb_state["wr"]], dbg[:, :], xb[:])
                ret_debug[0] = True
            return

        do_layer(0)
        if ret_debug[0]:
            return finish([s_dbg])
        if debug == "xb1":
            pass
        do_layer(1)
        if debug == "xb2":
            dstore(s_dbg, [xb_state["wr"]], dbg[:, :], xb[:])
            return finish([s_dbg])
        if debug == "fullplus":
            dstore(s_dbg, [xb_state["wr"]], dbg[:, 0:NDC * TB], xb[:])

        p.barrier()
        with contextlib.ExitStack() as s4:
            SBo = lambda n, shp: s4.enter_context(nc.sbuf_tensor(n, list(shp), F32))
            xt = SBo("xt", [128, 2 * D])
            a5t = SBo("a5t", [128, 2 * NDC * 128])
            ssum = SBo("ssum", [128, NDC * 128])
            outb = SBo("outb", [128, 2 * D])
            ps_f = s4.enter_context(nc.psum_tensor("ps_f", [128, 1024], F32))
            ot = {"s_free": None, "ps_free": None, "out_free": [None, None], "a5_free": [None, None]}
            sox = [stream_sem(), stream_sem()]
            soa = [stream_sem(), stream_sem()]
            for t in range(TOWN // 128):
                sl = t % 2
                ld.wait(ot["out_free"][sl])
                lx = ld.emitm(sox[sl], lambda e, t=t, sl=sl: e.dma_start(
                    out=xt[:, sl * D:(sl + 1) * D], in_=x_attn[t * 128:(t + 1) * 128, :]))
                ld.wait(a5_store_all, a5_store_all1, ot["a5_free"][sl])
                la = ld.emitm(soa[sl], lambda e, t=t, sl=sl: e.dma_start(
                    out=a5t[:, sl * NDC * 128:(sl + 1) * NDC * 128].rearrange("p (dc c) -> p dc c", c=128),
                    in_=a5_hbm.rearrange("(dc p) q -> p dc q", p=128)[:, :, t * 128:(t + 1) * 128]))
                ve.wait(lx, la, ot["s_free"], xb_state["wr"])
                sv = ve.emitm(s_ve, lambda e, t=t, sl=sl: e.tensor_add(
                    ssum[:].rearrange("p (dc c) -> p dc c", c=128),
                    a5t[:, sl * NDC * 128:(sl + 1) * NDC * 128].rearrange("p (dc c) -> p dc c", c=128),
                    fm_ap(xb, TB, (OWN0 + t) * 128, 128)))
                ot["a5_free"][sl] = sv
                if debug == "o1":
                    dstore(s_dbg, [sv], dbg[:, :], ssum[:])
                    break
                te.wait(sv, i_tok, ot["ps_free"])
                for dc in range(NDC):
                    fn = lambda e, dc=dc: e.transpose(ps_f[:, dc * 128:(dc + 1) * 128], ssum[:, dc * 128:(dc + 1) * 128], ident[:])
                    tf = te.emitm(s_te, fn) if dc == NDC - 1 else te.emit(fn)
                ot["s_free"] = tf
                ve.wait(tf)
                if debug == "o2":
                    ov = ve.emitm(s_ve, lambda e: e.tensor_copy(outb[:, 0:D], ps_f[:]))
                    dstore(s_dbg, [ov], dbg[:, :], outb[:, 0:D])
                    break
                ov = ve.emitm(s_ve, lambda e, sl=sl: e.scalar_tensor_tensor(
                    outb[:, sl * D:(sl + 1) * D], ps_f[:], 1.0, xt[:, sl * D:(sl + 1) * D], ALU.mult, ALU.add))
                ot["ps_free"] = ov
                if debug == "o3":
                    dstore(s_dbg, [ov], dbg[:, :], outb[:, sl * D:(sl + 1) * D])
                    break
                dstore(s_sout[sl], [ov], out_h[t * 128:(t + 1) * 128, :], outb[:, sl * D:(sl + 1) * D])
                ot["out_free"][sl] = (s_sout[sl].idx, s_sout[sl].count)
                if debug == "o4" and t == 1:
                    dstore(s_dbg, [ov], dbg[:, :], outb[:, sl * D:(sl + 1) * D])
                    break
                if debug == "o5" and t == 3:
                    dstore(s_dbg, [ov], dbg[:, :], outb[:, sl * D:(sl + 1) * D])
                    break
                if debug == "o6" and t == 7:
                    dstore(s_dbg, [ov], dbg[:, :], outb[:, sl * D:(sl + 1) * D])
                    break

    if debug in ("o1", "o2", "o3", "o4", "o5", "o6"):
        return finish([s_dbg])
    if debug == "fullplus":
        return finish([s_sout[0], s_sout[1], s_dbg])
    return finish([s_sout[0], s_sout[1]])



_TAB_CACHE = {}


def _rotary_tables_f32():
    if "t" in _TAB_CACHE:
        return _TAB_CACHE["t"]
    import jax
    import jax.numpy as jnp
    cpu = jax.devices("cpu")[0]
    with jax.default_device(cpu):
        half = jnp.arange(0, DH, 2, dtype=jnp.float32)
        inv_freq = 1.0 / 10000.0 ** (half / DH)
        t = jnp.arange(S, dtype=jnp.float32)
        freqs = t[:, None] * inv_freq[None, :]
        freqs = jnp.concatenate([freqs, freqs], axis=-1)
        base = (half + 0.4 * DH) / (1.4 * DH)
        power = (t - S // 2) / float(WIN // 2)
        scale = base[None, :] ** power[:, None]
        scale = jnp.concatenate([scale, scale], axis=-1)
        cos, sin = jnp.cos(freqs), jnp.sin(freqs)
        res = (np.asarray(cos, np.float32), np.asarray(sin, np.float32),
               np.asarray(scale, np.float32))
    _TAB_CACHE["t"] = res
    return res


def _prep_core_inputs(inputs, b, h):
    x = np.asarray(inputs["L0_attn_inputs"][b], dtype=np.float32)
    own = slice(h * TOWN, (h + 1) * TOWN)
    other = slice((1 - h) * TOWN, (2 - h) * TOWN)
    x_attn = np.ascontiguousarray(np.concatenate([x[own], x[other]], axis=0))

    base0 = h * TOWN - OWN0 * 128
    x_bwin = np.zeros((TB, D), dtype=np.float32)
    lo, hi = max(0, base0), min(S, base0 + TB)
    x_bwin[lo - base0: hi - base0] = x[lo:hi]

    cos_f, sin_f, scale_f = _rotary_tables_f32()
    tidx = np.clip(base0 + np.arange(TB), 0, S - 1)
    cos_t = np.ascontiguousarray(cos_f[tidx])
    sin_t = np.ascontiguousarray(sin_f[tidx])
    scale = np.ascontiguousarray(scale_f[tidx])

    def mk_masks(t0):
        n = NT - t0
        m = np.zeros((n, 128, NKB), dtype=np.float32)
        a = np.arange(128)[:, None]
        j = np.arange(NKB)[None, :]
        band = (j < a) | (j > a + WIN)
        for i in range(n):
            t = t0 + i
            abs_start = base0 + (t - 4) * 128
            kill = (j + abs_start) < 0
            m[i][band | kill] = NEG
        return m

    gam = np.asarray(inputs["L0_attn_gamma"], dtype=np.float32)
    bet = np.asarray(inputs["L0_attn_beta"], dtype=np.float32)
    gbat = np.ascontiguousarray(np.stack([gam[0], bet[0], gam[1], bet[1]], axis=1))

    lng = np.zeros((128, 32), dtype=np.float32)
    ln_g = np.asarray(inputs["L0_ffn_ln_g"], dtype=np.float32)
    ln_b = np.asarray(inputs["L0_ffn_ln_b"], dtype=np.float32)
    for l in range(2):
        for dc in range(NDC):
            lng[:, l * 16 + dc * 2 + 0] = ln_g[l, dc * 128:(dc + 1) * 128]
            lng[:, l * 16 + dc * 2 + 1] = ln_b[l, dc * 128:(dc + 1) * 128]

    sigma = np.concatenate([np.arange(32, 64), np.arange(0, 32)])
    eps_pat = np.concatenate([-np.ones(32), np.ones(32)]).astype(np.float32)
    qs = np.asarray(inputs["L0_ffn_q_scale"], dtype=np.float32)
    ks = np.asarray(inputs["L0_ffn_k_scale"], dtype=np.float32)

    return {
        "x_attn": x_attn, "x_bwin": x_bwin,
        "w_uv": np.asarray(inputs["L0_attn_W_uv"], np.float32),
        "b_uv": np.asarray(inputs["L0_attn_b_uv"], np.float32)[None, :],
        "gbat": gbat,
        "w_out": np.asarray(inputs["L0_attn_W_out"], np.float32),
        "b_out": np.asarray(inputs["L0_attn_b_out"], np.float32)[None, :],
        "wqkv": np.asarray(inputs["L0_ffn_wqkv"], np.float32),
        "wo": np.asarray(inputs["L0_ffn_wo"], np.float32),
        "w1": np.asarray(inputs["L0_ffn_w1"], np.float32),
        "b1": np.asarray(inputs["L0_ffn_b1"], np.float32),
        "w2": np.asarray(inputs["L0_ffn_w2"], np.float32),
        "b2": np.asarray(inputs["L0_ffn_b2"], np.float32),
        "lngb": lng,
        "qs_row": qs, "ks_row": ks,
        "qs_sh": np.ascontiguousarray(qs[:, sigma]),
        "ks_sh": np.ascontiguousarray(ks[:, sigma]),
        "eps_row": eps_pat[None, :],
        "cos_t": cos_t, "sin_t": sin_t, "scale_t": scale,
        "masks0": mk_masks(ATTN_T[0]), "masks1": mk_masks(ATTN_T[1]),
        "ident": np.eye(128, dtype=np.float32),
        "kill01": np.full((128, 1), 0.0 if base0 < 0 else 1.0, dtype=np.float32),
    }


def kernel(**inputs):
    nc = build_full()
    in_maps = [_prep_core_inputs(inputs, c // 2, c % 2) for c in range(8)]
    res = run_bass_kernel_spmd(nc, in_maps, core_ids=list(range(8)))
    out = np.zeros((B, S, D), dtype=np.asarray(res.results[0]["out"]).dtype)
    for c in range(8):
        b, h = c // 2, c % 2
        out[b, h * TOWN:(h + 1) * TOWN] = res.results[c]["out"]
    return out
